# version 05 (final)
# baseline (speedup 1.0000x reference)
import contextlib
import numpy as np
import concourse.bass as bass
import concourse.mybir as mybir
from concourse.bass_utils import run_bass_kernel_spmd

F32 = mybir.dt.float32
BF16 = mybir.dt.bfloat16
AF = mybir.ActivationFunctionType
ALU = mybir.AluOpType
AX = mybir.AxisListType

NCORES = 8
B = 4
T = 4096
TC = T // 2
D = 2048
DI = 4096
NST = 16
DH = DI // 2

C_ZERO, C_ONE, C_EPS_LN, C_NCONST = 0, 1, 2, 3


class Prog:

    ENG = ("tensor", "vector", "scalar", "gpsimd", "sync")
    QENG = {"dsync": "sync", "dgps": "gpsimd", "dscal": "scalar", "dvec": "vector"}

    def __init__(self):
        self.nc = bass.Bass("TRN2", target_bir_lowering=False)
        self.es = contextlib.ExitStack()
        self.sem = {}
        self.cnt = {}
        for e in self.ENG:
            self.sem[e] = self.es.enter_context(self.nc.semaphore("s_" + e))
            self.cnt[e] = 0
        for q in self.QENG:
            self.sem[q] = self.es.enter_context(self.nc.semaphore("s_" + q))
            self.cnt[q] = 0
        self._names = {}
        self._ext = {}
        self.cur = self.es

    def _uname(self, nm):
        k = self._names.get(nm, 0)
        self._names[nm] = k + 1
        return nm if k == 0 else f"{nm}_{k}"

    def sbuf(self, nm, shape, dt=F32):
        return self.cur.enter_context(self.nc.sbuf_tensor(self._uname(nm), shape, dt))

    def psum(self, nm, shape, dt=F32):
        return self.cur.enter_context(self.nc.psum_tensor(self._uname(nm), shape, dt))

    def din(self, nm, shape, dt=F32):
        return self.nc.dram_tensor(nm, shape, dt, kind="ExternalInput").ap()

    def dout(self, nm, shape, dt=F32):
        return self.nc.dram_tensor(nm, shape, dt, kind="ExternalOutput").ap()

    def idram(self, nm, shape, dt=F32):
        return self.nc.dram_tensor(nm, shape, dt, kind="Internal").ap()

    def bind(self, nm, ap):
        self._ext[nm] = ap

    def ein(self, nm, shape, dt=F32):
        if nm not in self._ext:
            self._ext[nm] = self.din(nm, shape, dt)
        return self._ext[nm]

    def eout(self, nm, shape, dt=F32):
        if nm not in self._ext:
            self._ext[nm] = self.dout(nm, shape, dt)
        return self._ext[nm]

    def hard_barrier(self):
        for s in list(self.QENG) + list(self.ENG):
            if self.cnt[s]:
                for e2 in self.ENG:
                    self._eng(e2).wait_ge(self.sem[s], self.cnt[s])
        self.nc.all_engine_barrier()

    @contextlib.contextmanager
    def stage(self):
        prev = self.cur
        self.cur = contextlib.ExitStack()
        try:
            yield
        finally:
            self.cur.close()
            self.cur = prev
            self.hard_barrier()

    def _eng(self, e):
        return getattr(self.nc, e)

    def op(self, e, meth, *args, **kw):
        eng = self._eng(e)
        if self.cnt[e] and e != "tensor":
            eng.wait_ge(self.sem[e], self.cnt[e])
        instr = getattr(eng, meth)(*args, **kw)
        instr.then_inc(self.sem[e], 1)
        self.cnt[e] += 1
        return instr

    def T(self, meth, *a, **k): return self.op("tensor", meth, *a, **k)
    def V(self, meth, *a, **k): return self.op("vector", meth, *a, **k)
    def S(self, meth, *a, **k): return self.op("scalar", meth, *a, **k)
    def G(self, meth, *a, **k): return self.op("gpsimd", meth, *a, **k)

    def wait(self, e, prod, at=None):
        v = self.cnt[prod] if at is None else at
        if v:
            self._eng(e).wait_ge(self.sem[prod], v)

    def dma(self, q, out, in_, **kw):
        eng = self._eng(self.QENG[q])
        eng.dma_start(out=out, in_=in_, **kw).then_inc(self.sem[q], 16)
        self.cnt[q] += 16
        return self.cnt[q]

    def barrier(self):
        for q in self.QENG:
            if self.cnt[q]:
                self._eng(self.QENG[q]).wait_ge(self.sem[q], self.cnt[q])
        self.nc.all_engine_barrier()

    def finish(self):
        self.es.close()
        return self.nc



def make_helpers(p, tblk=512, ln=True, pool=None):
    h = {"tblk": tblk}
    h["consts"] = p.sbuf("h_consts", [128, C_NCONST], F32)
    h["ones_k"] = p.sbuf("h_ones_k", [128, 1], F32)
    h["ones_m"] = p.sbuf("h_ones_m", [1, 128], F32)
    if ln:
        h["sq"] = [p.sbuf("h_sq", [128, tblk], F32) for _ in range(2)]
        h["st"] = [p.sbuf("h_st", [1, tblk], F32) for _ in range(8)]
        if pool is None:
            h["ps_sx"] = p.psum("h_ps_sx", [1, tblk], F32)[:]
            h["ps_sq"] = p.psum("h_ps_sq", [1, tblk], F32)[:]
            pbc = p.psum("h_ps_bc", [128, 2 * tblk], F32)
            h["ps_bca"] = pbc[:, 0:tblk]
            h["ps_bcb"] = pbc[:, tblk:2 * tblk]
        else:
            h["ps_sx"] = pool[0][0:1, :]
            h["ps_sq"] = pool[1][0:1, :]
            h["ps_bca"] = pool[2][:]
            h["ps_bcb"] = pool[3][:]
    return h


def init_helpers(p, h, consts_ap):
    p.dma("dsync", h["consts"][:], consts_ap)
    p.G("memset", h["ones_k"][:], 1.0)
    p.G("memset", h["ones_m"][:], 1.0)


def emit_ln(p, h, x_chunks, out_chunks, nchunks, tblk, nd=None, gamma=None, beta=None,
            out_plain=None):
    nd = nd if nd is not None else nchunks * 128
    st, sq = h["st"], h["sq"]
    for ci in range(nchunks):
        p.T("matmul", h["ps_sx"], h["ones_k"][:], x_chunks[ci],
            start=(ci == 0), stop=(ci == nchunks - 1))
    for ci in range(nchunks):
        buf = sq[ci % 2]
        if ci >= 2:
            p.wait("scalar", "tensor")
        p.S("activation", out=buf[:], in_=x_chunks[ci], func=AF.Square)
        p.wait("tensor", "scalar")
        p.T("matmul", h["ps_sq"], h["ones_k"][:], buf[:],
            start=(ci == 0), stop=(ci == nchunks - 1))
    p.wait("scalar", "tensor")
    mu, ex2, a, b = st[0][:], st[1][:], st[2][:], st[3][:]
    p.S("mul", out=mu, in_=h["ps_sx"], mul=1.0 / nd)
    p.S("mul", out=ex2, in_=h["ps_sq"], mul=1.0 / nd)
    p.S("activation", out=st[4][:], in_=mu, func=AF.Square)
    p.wait("vector", "scalar")
    p.V("tensor_tensor", out=st[5][:], in0=ex2, in1=st[4][:], op=ALU.subtract)
    p.wait("scalar", "vector")
    p.S("activation", out=st[6][:], in_=st[5][:], func=AF.Sqrt,
        bias=h["consts"][0:1, C_EPS_LN:C_EPS_LN + 1])
    p.wait("vector", "scalar")
    p.V("reciprocal", out=a, in_=st[6][:])
    p.V("tensor_tensor", out=st[7][:], in0=mu, in1=a, op=ALU.mult)
    p.V("tensor_scalar", out=b, in0=st[7][:], scalar1=-1.0, scalar2=None, op0=ALU.mult)
    p.wait("tensor", "vector")
    p.T("matmul", h["ps_bca"], h["ones_m"][:], a, start=True, stop=True)
    p.T("matmul", h["ps_bcb"], h["ones_m"][:], b, start=True, stop=True)
    p.wait("vector", "tensor")
    for ci in range(nchunks):
        if out_chunks[ci] is None:
            continue
        p.V("tensor_tensor", out=sq[ci % 2][:], in0=x_chunks[ci],
            in1=h["ps_bca"], op=ALU.mult)
        if gamma is None:
            p.V("tensor_tensor", out=out_chunks[ci], in0=sq[ci % 2][:],
                in1=h["ps_bcb"], op=ALU.add)
        else:
            plain = out_plain[ci] if out_plain is not None else sq[ci % 2][:]
            p.V("tensor_tensor", out=plain, in0=sq[ci % 2][:],
                in1=h["ps_bcb"], op=ALU.add)
            p.V("tensor_scalar", out=out_chunks[ci], in0=plain,
                scalar1=gamma[ci], scalar2=beta[ci], op0=ALU.mult, op1=ALU.add)


def emit_softplus(p, h, out, x, scratch):
    p.wait("scalar", "vector")
    p.S("activation", out=scratch, in_=x, func=AF.Abs)
    p.S("activation", out=scratch, in_=scratch, func=AF.Exp, scale=-1.0)
    p.S("activation", out=scratch, in_=scratch, func=AF.Ln,
        bias=h["consts"][0:scratch.shape[0], C_ONE:C_ONE + 1])
    p.V("tensor_scalar", out=out, in0=x, scalar1=0.0, scalar2=None, op0=ALU.max)
    p.wait("vector", "scalar")
    p.V("tensor_tensor", out=out, in0=out, in1=scratch, op=ALU.add)



def emit_l0a_pre1(p):
    xin = p.ein("xin", [D, TC])
    wipT = p.ein("wipT", [D, 8192])
    consts = p.ein("consts", [128, C_NCONST])
    xz = p.eout("xz", [8192, TC])

    NT = TC // 512
    NI = D // 128
    NO = 8192 // 128

    h = make_helpers(p)
    init_helpers(p, h, consts)
    xc = [p.sbuf("xc", [128, 512], F32) for _ in range(NI)]
    ln = [p.sbuf("ln", [128, TC], BF16) for _ in range(NI)]
    wcol = [p.sbuf("wcol", [128, 16 * 128], BF16) for _ in range(2)]
    ps = [p.psum("ps_mm", [128, 512], F32) for _ in range(2)]
    ob = [p.sbuf("ob", [128, 512], F32) for _ in range(2)]
    p.barrier()

    for tb in range(NT):
        for ci in range(NI):
            p.dma("dsync", xc[ci][:], xin[ci * 128:(ci + 1) * 128, tb * 512:(tb + 1) * 512])
        p.wait("tensor", "dsync")
        p.wait("scalar", "dsync")
        p.wait("vector", "dsync")
        emit_ln(p, h, [xc[ci][:] for ci in range(NI)],
                [ln[ci][:, tb * 512:(tb + 1) * 512] for ci in range(NI)], NI, 512)
        p.barrier()

    def wsrc(oc):
        return wipT[:, oc * 128:(oc + 1) * 128].rearrange("(ic ip) o -> ip ic o", ip=128)

    ldc = {}
    tdone = {}
    evc = {}
    p.dma("dgps", wcol[0][:].rearrange("p (ic o) -> p ic o", o=128), wsrc(0))
    ldc[0] = p.cnt["dgps"]
    for oc in range(NO):
        buf = wcol[oc % 2]
        if oc + 1 < NO:
            if oc >= 1:
                p.wait("gpsimd", "tensor", at=tdone.get(oc - 1))
            p.dma("dgps", wcol[(oc + 1) % 2][:].rearrange("p (ic o) -> p ic o", o=128),
                  wsrc(oc + 1))
            ldc[oc + 1] = p.cnt["dgps"]
        p.wait("tensor", "dgps", at=ldc[oc])
        for tb in range(NT):
            psb, obb = ps[tb % 2], ob[tb % 2]
            if evc.get(tb % 2):
                p.wait("tensor", "vector", at=evc[tb % 2])
            for ic in range(NI):
                p.T("matmul", psb[:], buf[:, ic * 128:(ic + 1) * 128],
                    ln[ic][:, tb * 512:(tb + 1) * 512],
                    start=(ic == 0), stop=(ic == NI - 1))
            p.wait("vector", "tensor")
            p.wait("vector", "dsync")
            p.V("tensor_copy", out=obb[:], in_=psb[:])
            evc[tb % 2] = p.cnt["vector"]
            p.wait("sync", "vector")
            p.dma("dsync", xz[oc * 128:(oc + 1) * 128, tb * 512:(tb + 1) * 512], obb[:])
        tdone[oc] = p.cnt["tensor"]
    p.wait("sync", "dsync")



def emit_l0a_pre2(p):
    xzx = p.ein("xzx", [DI, TC + 3])
    convw = p.ein("convw", [128, DI // 128 * 4])
    convb = p.ein("convb", [128, DI // 128])
    xpT = p.ein("xpT", [DI, 160])
    dtpT = p.ein("dtpT", [128, DI])
    dtpb = p.ein("dtpb", [128, DI // 128])
    consts = p.ein("consts", [128, C_NCONST])
    xact = p.eout("xact", [DI, TC])
    uu = p.eout("uu", [DI, TC])
    bc = p.eout("bc", [32, TC])
    dtv = p.eout("dtv", [DI, TC])

    NC = DI // 128
    NT = TC // 512

    h = make_helpers(p, ln=False)
    init_helpers(p, h, consts)

    xb = [p.sbuf("xb", [128, TC + 3], F32) for _ in range(2)]
    cwt = p.sbuf("cwt", [128, 4 * NC], F32)
    cbt = p.sbuf("cbt", [128, NC], F32)
    dbt = p.sbuf("dbt", [128, NC], F32)
    acc = p.sbuf("acc", [128, TC], F32)
    xa = [p.sbuf("xa", [128, TC], BF16) for _ in range(NC)]
    xaf = p.sbuf("xaf", [128, TC], F32)
    xpw = p.sbuf("xpw", [128, 160], BF16)
    ps_dt = [p.psum("ps_dt", [128, 512], F32) for _ in range(NT)]
    dtr = p.sbuf("dtr", [128, TC], BF16)
    bcs = p.sbuf("bcs", [32, TC], F32)
    xpw2 = [p.sbuf("xpw2", [128, 32], BF16) for _ in range(2)]
    dtw = [p.sbuf("dtw", [128, 128], BF16) for _ in range(2)]
    dvb = [p.sbuf("dvb", [128, TC], F32) for _ in range(3)]
    xrb = p.sbuf("xrb", [128, TC], F32)

    p.dma("dsync", cwt[:], convw)
    p.dma("dsync", cbt[:], convb)
    p.dma("dsync", dbt[:], dtpb)
    p.barrier()

    for ci in range(NC):
        buf = xb[ci % 2]
        if ci >= 2:
            p.wait("sync", "vector")
        p.dma("dsync", buf[:], xzx[ci * 128:(ci + 1) * 128, :])
        p.wait("vector", "dsync")
        p.V("tensor_scalar", out=acc[:], in0=buf[:, 0:TC],
            scalar1=cwt[:, 4 * ci:4 * ci + 1], scalar2=cbt[:, ci:ci + 1],
            op0=ALU.mult, op1=ALU.add)
        for k in range(1, 4):
            p.V("scalar_tensor_tensor", out=acc[:], in0=buf[:, k:k + TC],
                scalar=cwt[:, 4 * ci + k:4 * ci + k + 1], in1=acc[:],
                op0=ALU.mult, op1=ALU.add)
        p.wait("scalar", "vector")
        p.wait("scalar", "dsync")
        p.S("activation", out=xaf[:], in_=acc[:], func=AF.Silu)
        p.S("activation", out=xa[ci][:], in_=acc[:], func=AF.Silu)
        p.wait("sync", "scalar")
        p.dma("dsync", xact[ci * 128:(ci + 1) * 128, :], xaf[:])
        p.wait("gpsimd", "tensor")
        p.dma("dgps", xpw[:], xpT[ci * 128:(ci + 1) * 128, :])
        p.wait("tensor", "dgps")
        p.wait("tensor", "scalar")
        for tb in range(NT):
            p.T("matmul", ps_dt[tb][:], xpw[:, 0:128],
                xa[ci][:, tb * 512:(tb + 1) * 512],
                start=(ci == 0), stop=(ci == NC - 1))
    p.wait("vector", "tensor")
    for tb in range(NT):
        p.V("tensor_copy", out=dtr[:, tb * 512:(tb + 1) * 512], in_=ps_dt[tb][:])
    p.barrier()

    for ci in range(NC):
        buf = xpw2[ci % 2]
        if ci >= 2:
            p.wait("gpsimd", "tensor")
        p.dma("dgps", buf[:], xpT[ci * 128:(ci + 1) * 128, 128:160])
        p.wait("tensor", "dgps")
        for tb in range(NT):
            p.T("matmul", ps_dt[tb][0:32, :], buf[:],
                xa[ci][:, tb * 512:(tb + 1) * 512],
                start=(ci == 0), stop=(ci == NC - 1))
    p.wait("vector", "tensor")
    for tb in range(NT):
        p.V("tensor_copy", out=bcs[:, tb * 512:(tb + 1) * 512], in_=ps_dt[tb][0:32, :])
    p.wait("sync", "vector")
    p.dma("dsync", bc[:, :], bcs[:, :])
    p.barrier()

    for dc in range(NC):
        buf = dtw[dc % 2]
        if dc >= 2:
            p.wait("gpsimd", "tensor")
        p.dma("dgps", buf[:], dtpT[:, dc * 128:(dc + 1) * 128])
        p.wait("scalar", "vector")
        p.dma("dscal", xrb[:], xact[dc * 128:(dc + 1) * 128, :])
        p.wait("tensor", "dgps")
        for tb in range(NT):
            pp = ps_dt[tb % 2]
            if dc * NT + tb >= 2:
                p.wait("tensor", "vector")
            p.T("matmul", pp[:], buf[:], dtr[:, tb * 512:(tb + 1) * 512],
                start=True, stop=True)
            p.wait("vector", "tensor")
            p.wait("vector", "dsync")
            p.V("tensor_scalar", out=dvb[0][:, tb * 512:(tb + 1) * 512], in0=pp[:],
                scalar1=dbt[:, dc:dc + 1], scalar2=None, op0=ALU.add)
        emit_softplus(p, h, dvb[1][:], dvb[0][:], dvb[2][:])
        p.wait("sync", "vector")
        p.dma("dsync", dtv[dc * 128:(dc + 1) * 128, :], dvb[1][:])
        p.wait("vector", "dscal")
        p.V("tensor_tensor", out=dvb[2][:], in0=dvb[1][:], in1=xrb[:], op=ALU.mult)
        p.wait("sync", "vector")
        p.dma("dsync", uu[dc * 128:(dc + 1) * 128, :], dvb[2][:])
    p.wait("sync", "dsync")



def emit_l0a_scan(p):
    dtv_h = p.ein("dtv_h", [DH, T])
    uu_h = p.ein("uu_h", [DH, T])
    bc_all = p.ein("bc_all", [32, T])
    alog_l = p.ein("alog_l", [128, 256])
    sel16 = p.ein("sel16", [128, 16 * 128])
    ysc = p.eout("ysc", [DH, T])

    NLB = DH * NST // 128

    asb = p.sbuf("asb", [128, 256], F32)
    asb2 = p.sbuf("asb2", [128, 256], F32)
    sel = p.sbuf("sel", [128, 16 * 128], F32)
    brep = p.sbuf("brep", [128, T], F32)
    crep = p.sbuf("crep", [128, T], F32)
    dtg = [p.sbuf("dtg", [128, T], F32) for _ in range(2)]
    uug = [p.sbuf("uug", [128, T], F32) for _ in range(2)]
    dA = p.sbuf("dA", [128, T], F32)
    dBx = p.sbuf("dBx", [128, T], F32)
    hh = p.sbuf("hh", [128, T], F32)
    hc = p.sbuf("hc", [128, T], F32)
    ps_y = [p.psum("ps_y", [128, 512], F32) for _ in range(8)]
    yev = p.sbuf("yev", [128, T], F32)

    p.dma("dsync", asb[:], alog_l)
    p.dma("dsync", sel[:], sel16)
    p.dma("dsync", brep[:], bc_all[0:16, :].unsqueeze(0).broadcast_to((8, NST, T)))
    p.dma("dsync", crep[:], bc_all[16:32, :].unsqueeze(0).broadcast_to((8, NST, T)))
    p.barrier()
    p.S("activation", out=asb2[:], in_=asb[:], func=AF.Exp)
    p.wait("vector", "scalar")
    p.V("tensor_scalar", out=asb[:], in0=asb2[:], scalar1=-1.0, scalar2=None, op0=ALU.mult)
    p.barrier()

    def rep_src(lb, arr):
        d0 = lb * 8
        return arr[d0:d0 + 8, :].unsqueeze(1).broadcast_to((8, NST, T))

    for lb in range(NLB):
        gb, ub = dtg[lb % 2], uug[lb % 2]
        p.wait("sync", "scalar")
        p.wait("gpsimd", "gpsimd")
        p.dma("dsync", gb[:], rep_src(lb, dtv_h))
        p.dma("dgps", ub[:], rep_src(lb, uu_h))
        p.wait("scalar", "dsync")
        p.wait("scalar", "vector")
        p.S("activation", out=dA[:], in_=gb[:], func=AF.Exp,
            scale=asb[:, lb:lb + 1])
        p.wait("gpsimd", "dgps")
        p.wait("gpsimd", "vector")
        p.G("tensor_tensor", out=dBx[:], in0=ub[:], in1=brep[:], op=ALU.mult)
        p.wait("vector", "scalar")
        p.wait("vector", "gpsimd")
        p.V("tensor_tensor_scan", out=hh[:], data0=dA[:], data1=dBx[:],
            initial=0.0, op0=ALU.mult, op1=ALU.add)
        p.wait("gpsimd", "vector")
        p.wait("gpsimd", "tensor")
        p.G("tensor_tensor", out=hc[:], in0=hh[:], in1=crep[:], op=ALU.mult)
        kk = lb % 16
        p.wait("tensor", "gpsimd")
        p.wait("tensor", "vector")
        for tb in range(8):
            p.T("matmul", ps_y[tb][:], sel[:, kk * 128:(kk + 1) * 128],
                hc[:, tb * 512:(tb + 1) * 512], start=(kk == 0), stop=(kk == 15))
        if kk == 15:
            dchunk = lb // 16
            p.wait("vector", "tensor")
            p.wait("vector", "dsync")
            for tb in range(8):
                p.V("tensor_copy", out=yev[:, tb * 512:(tb + 1) * 512], in_=ps_y[tb][:])
            p.wait("sync", "vector")
            p.dma("dsync", ysc[dchunk * 128:(dchunk + 1) * 128, :], yev[:])
    p.wait("sync", "dsync")



def emit_l0a_post(p):
    ysc_t = p.ein("ysc_t", [DI, TC])
    xact_t = p.ein("xact_t", [DI, TC])
    zz = p.ein("zz", [DI, TC])
    dvec = p.ein("dvec", [128, DI // 128])
    wopT = p.ein("wopT", [DI, D])
    xf = p.ein("xf", [D, TC])
    xf1 = p.eout("xf1", [D, TC])

    NK = DI // 128
    NM = D // 128

    dsb = p.sbuf("dsb", [128, NK], F32)
    ya = [p.sbuf("ya", [128, 512], F32) for _ in range(2)]
    xab = [p.sbuf("xab", [128, 512], F32) for _ in range(2)]
    zb = [p.sbuf("zb", [128, 512], F32) for _ in range(2)]
    sz = p.sbuf("sz", [128, 512], F32)
    yg = [p.sbuf("yg", [128, TC], BF16) for _ in range(NK)]
    wcol = p.sbuf("wcol2", [128, NK * 128], BF16)
    ps = [p.psum("ps_o", [128, 512], F32) for _ in range(2)]
    xfb = [p.sbuf("xfb", [128, 512], F32) for _ in range(2)]
    ob = [p.sbuf("ob2", [128, 512], F32) for _ in range(2)]

    p.dma("dsync", dsb[:], dvec)
    p.barrier()

    for kc in range(NK):
        for tb in range(4):
            i = (kc * 4 + tb) % 2
            sl = slice(tb * 512, (tb + 1) * 512)
            p.wait("sync", "vector")
            p.wait("gpsimd", "vector")
            p.dma("dsync", ya[i][:], ysc_t[kc * 128:(kc + 1) * 128, sl])
            p.dma("dgps", xab[i][:], xact_t[kc * 128:(kc + 1) * 128, sl])
            p.dma("dscal", zb[i][:], zz[kc * 128:(kc + 1) * 128, sl])
            p.wait("vector", "dsync")
            p.wait("vector", "dgps")
            p.V("scalar_tensor_tensor", out=ya[i][:], in0=xab[i][:],
                scalar=dsb[:, kc:kc + 1], in1=ya[i][:], op0=ALU.mult, op1=ALU.add)
            p.wait("scalar", "dscal")
            p.wait("scalar", "vector")
            p.S("activation", out=sz[:], in_=zb[i][:], func=AF.Silu)
            p.wait("vector", "scalar")
            p.V("tensor_tensor", out=yg[kc][:, sl], in0=ya[i][:], in1=sz[:], op=ALU.mult)
    p.barrier()

    for mc in range(NM):
        src = wopT[:, mc * 128:(mc + 1) * 128].rearrange("(kc kp) m -> kp kc m", kp=128)
        p.dma("dgps", wcol[:].rearrange("p (kc m) -> p kc m", m=128), src)
        p.wait("tensor", "dgps")
        for tb in range(4):
            pp = ps[tb % 2]
            sl = slice(tb * 512, (tb + 1) * 512)
            if mc * 4 + tb >= 2:
                p.wait("tensor", "vector")
            for kc in range(NK):
                p.T("matmul", pp[:], wcol[:, kc * 128:(kc + 1) * 128],
                    yg[kc][:, sl], start=(kc == 0), stop=(kc == NK - 1))
            i = tb % 2
            p.wait("sync", "vector")
            p.dma("dsync", xfb[i][:], xf[mc * 128:(mc + 1) * 128, sl])
            p.wait("vector", "dsync")
            p.wait("vector", "tensor")
            p.V("tensor_tensor", out=ob[i][:], in0=pp[:], in1=xfb[i][:], op=ALU.add)
            p.wait("sync", "vector")
            p.dma("dsync", xf1[mc * 128:(mc + 1) * 128, sl], ob[i][:])
        p.barrier()
    p.wait("sync", "dsync")



def _consts_host():
    c = np.zeros((128, C_NCONST), np.float32)
    c[:, C_ONE] = 1.0
    c[:, C_EPS_LN] = 1e-5
    return c


def _sel16_host():
    s = np.zeros((128, 16 * 128), np.float32)
    for kk in range(16):
        for lane in range(128):
            s[lane, kk * 128 + kk * 8 + lane // 16] = 1.0
    return np.ascontiguousarray(s)


def _scatter_tok(dst, results, key):
    for c in range(NCORES):
        b, hh = c // 2, c % 2
        dst[b, :, hh * TC:(hh + 1) * TC] = results[c][key]


def run_l0a(xf, inp):
    consts = _consts_host()
    wipT = np.ascontiguousarray(inp["L0_attn_a_in_proj_w"].T)
    nc1 = build_l0a_pre1()
    maps = [{"xin": np.ascontiguousarray(xf[c // 2, :, (c % 2) * TC:(c % 2 + 1) * TC]),
             "wipT": wipT, "consts": consts} for c in range(NCORES)]
    r1 = run_bass_kernel_spmd(nc1, maps, core_ids=list(range(NCORES)))
    xz = np.zeros((B, 8192, T), np.float32)
    _scatter_tok(xz, r1.results, "xz")

    nc2 = build_l0a_pre2()
    convw = np.ascontiguousarray(
        inp["L0_attn_a_conv_w"].reshape(32, 128, 4).transpose(1, 0, 2).reshape(128, 128))
    convb = np.ascontiguousarray(inp["L0_attn_a_conv_b"].reshape(32, 128).T)
    xpT = np.ascontiguousarray(inp["L0_attn_a_x_proj_w"].T)
    dtpT = np.ascontiguousarray(inp["L0_attn_a_dt_proj_w"].T)
    dtpb = np.ascontiguousarray(inp["L0_attn_a_dt_proj_b"].reshape(32, 128).T)
    xpad = np.zeros((B, DI, T + 3), np.float32)
    xpad[:, :, 3:] = xz[:, 0:DI, :]
    maps = []
    for c in range(NCORES):
        b, hh = c // 2, c % 2
        t0 = hh * TC
        maps.append({"xzx": np.ascontiguousarray(xpad[b, :, t0:t0 + TC + 3]),
                     "convw": convw, "convb": convb, "xpT": xpT,
                     "dtpT": dtpT, "dtpb": dtpb, "consts": consts})
    r2 = run_bass_kernel_spmd(nc2, maps, core_ids=list(range(NCORES)))
    xact = np.zeros((B, DI, T), np.float32)
    uu = np.zeros((B, DI, T), np.float32)
    bc = np.zeros((B, 32, T), np.float32)
    dtv = np.zeros((B, DI, T), np.float32)
    _scatter_tok(xact, r2.results, "xact")
    _scatter_tok(uu, r2.results, "uu")
    _scatter_tok(bc, r2.results, "bc")
    _scatter_tok(dtv, r2.results, "dtv")

    nc3 = build_l0a_scan()
    alog = np.asarray(inp["L0_attn_a_A_log"], np.float32)
    sel16 = _sel16_host()
    maps = []
    for c in range(NCORES):
        b, dh = c // 2, c % 2
        rows = slice(dh * DH, (dh + 1) * DH)
        al = np.ascontiguousarray(alog[rows].reshape(256, 128).T)
        maps.append({"dtv_h": np.ascontiguousarray(dtv[b, rows]),
                     "uu_h": np.ascontiguousarray(uu[b, rows]),
                     "bc_all": np.ascontiguousarray(bc[b]),
                     "alog_l": al, "sel16": sel16})
    r3 = run_bass_kernel_spmd(nc3, maps, core_ids=list(range(NCORES)))
    ysc = np.zeros((B, DI, T), np.float32)
    for c in range(NCORES):
        b, dh = c // 2, c % 2
        ysc[b, dh * DH:(dh + 1) * DH, :] = r3.results[c]["ysc"]

    nc4 = build_l0a_post()
    wopT = np.ascontiguousarray(inp["L0_attn_a_out_proj_w"].T)
    dvec = np.ascontiguousarray(inp["L0_attn_a_D"].reshape(32, 128).T)
    maps = []
    for c in range(NCORES):
        b, hh = c // 2, c % 2
        sl = slice(hh * TC, (hh + 1) * TC)
        maps.append({"ysc_t": np.ascontiguousarray(ysc[b, :, sl]),
                     "xact_t": np.ascontiguousarray(xact[b, :, sl]),
                     "zz": np.ascontiguousarray(xz[b, DI:2 * DI, sl]),
                     "dvec": dvec, "wopT": wopT,
                     "xf": np.ascontiguousarray(xf[b, :, sl])})
    r4 = run_bass_kernel_spmd(nc4, maps, core_ids=list(range(NCORES)))
    xf1 = np.zeros_like(xf)
    _scatter_tok(xf1, r4.results, "xf1")
    return xf1



def emit_l0ffn(p):
    xf = p.ein("xf", [D, TC])
    wT = p.ein("wT", [D, D])
    biast = p.ein("biast", [128, 16])
    consts = p.ein("consts", [128, C_NCONST])
    xo = p.eout("xo", [D, TC])

    NI = D // 128
    NT = TC // 512

    h = make_helpers(p)
    init_helpers(p, h, consts)
    bt = p.sbuf("bt", [128, 16], F32)
    arena = p.sbuf("arena", [128, 8192], F32)
    xc = [arena[:, ci * 512:(ci + 1) * 512] for ci in range(NI)]
    ln = [p.sbuf("ln", [128, TC], BF16) for _ in range(NI)]
    wbuf = [arena[:, 0:2048], arena[:, 2048:4096]]
    absb = [arena[:, 4096:6144], arena[:, 6144:8192]]
    ps_al = [p.psum("ps_al", [1, 512], F32) for _ in range(2)]
    alr = p.sbuf("alr", [1, D], F32)
    albc = arena[:, 6144:8192]
    bw = [p.sbuf("bw", [128, D], BF16) for _ in range(NI)]
    gt = arena[:, 4096:6144]
    ps = [p.psum("ps_mm", [128, 512], F32) for _ in range(2)]
    xfb = [p.sbuf("xfb", [128, 512], F32) for _ in range(2)]
    ob = [p.sbuf("ob", [128, 512], F32) for _ in range(2)]
    p.dma("dsync", bt[:], biast)
    p.barrier()

    for tb in range(NT):
        for ci in range(NI):
            p.dma("dsync", xc[ci], xf[ci * 128:(ci + 1) * 128, tb * 512:(tb + 1) * 512])
        p.wait("tensor", "dsync")
        p.wait("scalar", "dsync")
        p.wait("vector", "dsync")
        emit_ln(p, h, [xc[ci] for ci in range(NI)],
                [ln[ci][:, tb * 512:(tb + 1) * 512] for ci in range(NI)], NI, 512)
        p.barrier()

    alps = [h["ps_sx"], h["ps_sq"], ps_al[0][:], ps_al[1][:]]
    for ci in range(NI):
        buf = wbuf[ci % 2]
        ab = absb[ci % 2]
        if ci >= 2:
            p.wait("sync", "scalar")
        p.dma("dsync", buf, wT[ci * 128:(ci + 1) * 128, :])
        p.wait("scalar", "dsync")
        p.wait("scalar", "tensor")
        p.S("activation", out=ab, in_=buf, func=AF.Abs)
        p.wait("tensor", "scalar")
        for ob_i in range(4):
            p.T("matmul", alps[ob_i], h["ones_k"][:],
                ab[:, ob_i * 512:(ob_i + 1) * 512],
                start=(ci == 0), stop=(ci == NI - 1))
    p.wait("scalar", "tensor")
    for ob_i in range(4):
        p.S("mul", out=alr[0:1, ob_i * 512:(ob_i + 1) * 512], in_=alps[ob_i],
            mul=1.0 / D)
    p.wait("tensor", "scalar")
    for ob_i in range(4):
        pp = ps[ob_i % 2]
        if ob_i >= 2:
            p.wait("tensor", "vector")
        p.T("matmul", pp[:], h["ones_m"][:], alr[0:1, ob_i * 512:(ob_i + 1) * 512],
            start=True, stop=True)
        p.wait("vector", "tensor")
        p.V("tensor_copy", out=albc[:, ob_i * 512:(ob_i + 1) * 512], in_=pp[:])
    p.barrier()

    for ci in range(NI):
        buf = wbuf[ci % 2]
        if ci >= 2:
            p.wait("sync", "vector")
        p.dma("dsync", buf, wT[ci * 128:(ci + 1) * 128, :])
        p.wait("vector", "dsync")
        p.V("tensor_scalar", out=gt, in0=buf, scalar1=0.0, scalar2=2.0,
            op0=ALU.is_ge, op1=ALU.mult)
        p.V("scalar_tensor_tensor", out=bw[ci][:], in0=gt, scalar=-1.0,
            in1=albc, op0=ALU.add, op1=ALU.mult)
    p.barrier()

    for oc in range(NI):
        for tb in range(NT):
            pp = ps[tb % 2]
            sl = slice(tb * 512, (tb + 1) * 512)
            if oc * NT + tb >= 2:
                p.wait("tensor", "vector")
            for ic in range(NI):
                p.T("matmul", pp[:], bw[ic][:, oc * 128:(oc + 1) * 128],
                    ln[ic][:, sl], start=(ic == 0), stop=(ic == NI - 1))
            i = tb % 2
            p.wait("sync", "vector")
            p.dma("dsync", xfb[i][:], xf[oc * 128:(oc + 1) * 128, sl])
            p.wait("vector", "dsync")
            p.wait("vector", "tensor")
            p.V("scalar_tensor_tensor", out=ob[i][:], in0=pp[:],
                scalar=bt[:, oc:oc + 1], in1=xfb[i][:], op0=ALU.add, op1=ALU.add)
            p.wait("sync", "vector")
            p.dma("dsync", xo[oc * 128:(oc + 1) * 128, sl], ob[i][:])
    p.wait("sync", "dsync")



def emit_l0b_qkv(p):
    xf = p.ein("xf", [D, TC])
    wqT = p.ein("wqT", [D, D])
    wkT = p.ein("wkT", [D, D])
    wvT = p.ein("wvT", [D, D])
    consts = p.ein("consts", [128, C_NCONST])
    qn = p.eout("qn", [D, TC], BF16)
    kn = p.eout("kn", [D, TC], BF16)
    vn = p.eout("vn", [TC, D], BF16)

    NI = D // 128
    NT = TC // 512

    h = make_helpers(p)
    init_helpers(p, h, consts)
    arena = p.sbuf("arena", [128, 8192], F32)
    xc = [arena[:, ci * 512:(ci + 1) * 512] for ci in range(NI)]
    ln = [p.sbuf("ln", [128, TC], BF16) for _ in range(NI)]
    wv = [p.sbuf("wv", [128, D], BF16) for _ in range(NI)]
    wcol = [p.sbuf("wcol", [128, 16 * 128], BF16) for _ in range(2)]
    ps = [p.psum("ps_mm", [128, 512], F32) for _ in range(2)]
    qh = arena[:, 0:2048]
    sqs = arena[:, 2048:4096]
    ssr = p.sbuf("ssr", [1, TC], F32)
    qo = arena[:, 6144:7168].bitcast(BF16)
    vb = arena[:, 4096:6144]
    vss = p.sbuf("vss", [128, 16], F32)
    vo = arena[:, 7168:8192].bitcast(BF16)
    p.barrier()

    for tb in range(NT):
        for ci in range(NI):
            p.dma("dsync", xc[ci], xf[ci * 128:(ci + 1) * 128, tb * 512:(tb + 1) * 512])
        p.wait("tensor", "dsync")
        p.wait("scalar", "dsync")
        p.wait("vector", "dsync")
        emit_ln(p, h, [xc[ci] for ci in range(NI)],
                [ln[ci][:, tb * 512:(tb + 1) * 512] for ci in range(NI)], NI, 512)
        p.barrier()

    for ipass, (wT, outg) in enumerate(((wqT, qn), (wkT, kn))):
        for oc in range(NI):
            buf = wcol[oc % 2]
            if oc >= 2 or ipass > 0:
                p.wait("gpsimd", "tensor")
            src = wT[:, oc * 128:(oc + 1) * 128].rearrange("(ic ip) o -> ip ic o", ip=128)
            p.dma("dgps", buf[:].rearrange("p (ic o) -> p ic o", o=128), src)
            p.wait("tensor", "dgps")
            p.wait("tensor", "vector")
            for tb in range(NT):
                pp = ps[tb % 2]
                if tb >= 2:
                    p.wait("tensor", "vector")
                for ic in range(NI):
                    p.T("matmul", pp[:], buf[:, ic * 128:(ic + 1) * 128],
                        ln[ic][:, tb * 512:(tb + 1) * 512],
                        start=(ic == 0), stop=(ic == NI - 1))
                p.wait("vector", "tensor")
                p.V("tensor_copy", out=qh[:, tb * 512:(tb + 1) * 512], in_=pp[:])
            p.wait("scalar", "vector")
            p.wait("scalar", "tensor")
            p.S("activation", out=sqs, in_=qh, func=AF.Square)
            alps = [h["ps_sx"], h["ps_sq"]]
            for tb in range(NT):
                p.wait("tensor", "scalar")
                p.T("matmul", alps[tb % 2], h["ones_k"][:],
                    sqs[:, tb * 512:(tb + 1) * 512], start=True, stop=True)
                p.wait("scalar", "tensor")
                p.S("activation", out=ssr[0:1, tb * 512:(tb + 1) * 512],
                    in_=alps[tb % 2], func=AF.Sqrt)
            p.wait("vector", "scalar")
            p.V("tensor_scalar", out=ssr[0:1, :], in0=ssr[0:1, :], scalar1=1e-8,
                scalar2=None, op0=ALU.add)
            p.V("reciprocal", out=ssr[0:1, :], in_=ssr[0:1, :])
            p.wait("tensor", "vector")
            for tb in range(NT):
                pp = ps[tb % 2]
                p.T("matmul", pp[:], h["ones_m"][:], ssr[0:1, tb * 512:(tb + 1) * 512],
                    start=True, stop=True)
                p.wait("vector", "tensor")
                p.wait("vector", "dsync")
                p.V("tensor_tensor", out=qo[:, tb * 512:(tb + 1) * 512],
                    in0=qh[:, tb * 512:(tb + 1) * 512], in1=pp[:], op=ALU.mult)
            p.wait("sync", "vector")
            p.dma("dsync", outg[oc * 128:(oc + 1) * 128, :], qo)

    for ci in range(NI):
        p.wait("gpsimd", "tensor")
        p.dma("dgps", wv[ci][:], wvT[ci * 128:(ci + 1) * 128, :])
    p.wait("tensor", "dgps")
    for tt in range(TC // 128):
        p.wait("tensor", "vector")
        for og in range(4):
            pp = ps[og % 2]
            if og >= 2:
                p.wait("tensor", "vector")
            for ic in range(NI):
                p.T("matmul", pp[:], ln[ic][:, tt * 128:(tt + 1) * 128],
                    wv[ic][:, og * 512:(og + 1) * 512],
                    start=(ic == 0), stop=(ic == NI - 1))
            p.wait("vector", "tensor")
            p.V("tensor_copy", out=vb[:, og * 512:(og + 1) * 512], in_=pp[:])
        p.V("tensor_tensor", out=sqs[:, 0:D], in0=vb, in1=vb, op=ALU.mult)
        p.V("tensor_reduce", out=vss[:], in_=sqs[:, 0:D].rearrange(
            "p (hd dd) -> p hd dd", dd=128), axis=AX.X, op=ALU.add)
        p.wait("scalar", "vector")
        p.S("activation", out=vss[:], in_=vss[:], func=AF.Sqrt)
        p.wait("vector", "scalar")
        p.V("tensor_scalar", out=vss[:], in0=vss[:], scalar1=1e-8, scalar2=None,
            op0=ALU.add)
        p.V("reciprocal", out=vss[:], in_=vss[:])
        p.wait("vector", "dsync")
        p.V("tensor_tensor", out=vo.rearrange("p (hd dd) -> p hd dd", dd=128),
            in0=vb.rearrange("p (hd dd) -> p hd dd", dd=128),
            in1=vss[:].unsqueeze(2).broadcast_to((128, 16, 128)), op=ALU.mult)
        p.wait("sync", "vector")
        p.dma("dsync", vn[tt * 128:(tt + 1) * 128, :], vo)
    p.wait("sync", "dsync")



def emit_l0b_attn(p):
    NH8 = 8
    qn = p.ein("qn", [NH8 * 128, T], BF16)
    kn = p.ein("kn", [NH8 * 128, T], BF16)
    vn = p.ein("vn", [T, NH8 * 128], BF16)
    tauv = p.ein("tauv", [128, 1])
    cmask = p.ein("cmask", [128, 4 * 512], BF16)
    attno = p.eout("attno", [NH8 * 128, T])

    NQB = T // 512
    NKT = T // 128

    tau = p.sbuf("tau", [128, 1], F32)
    ntau = p.sbuf("ntau", [128, 1], F32)
    cm = p.sbuf("cm", [128, 4 * 512], BF16)
    onesb = p.sbuf("onesb", [128, 1], BF16)
    qh = p.sbuf("qh", [128, T], BF16)
    kh = p.sbuf("kh", [128, T], BF16)
    vh = [p.sbuf("vh", [128, NH8 * 128], BF16) for _ in range(NKT)]
    ps_s = p.psum("ps_s", [128, 512], F32)
    pt = p.sbuf("pt", [128, 512], BF16)
    ps_n = p.psum("ps_n", [128, 512], F32)
    ps_d = p.psum("ps_d", [1, 512], F32)
    dnr = p.sbuf("dnr", [1, 512], F32)
    ps_b = p.psum("ps_b", [128, 512], F32)
    nev = p.sbuf("nev", [128, 512], F32)
    oev = p.sbuf("oev", [128, 512], F32)

    onesm = p.sbuf("onesm", [1, 128], F32)
    p.G("memset", onesm[:], 1.0)
    p.dma("dsync", tau[:], tauv)
    p.dma("dsync", cm[:], cmask)
    for kt in range(NKT):
        p.dma("dgps", vh[kt][:], vn[kt * 128:(kt + 1) * 128, :])
    p.G("memset", onesb[:], 1.0)
    p.barrier()
    p.S("activation", out=ntau[:], in_=tau[:], func=AF.Abs)
    p.wait("vector", "scalar")
    p.V("tensor_scalar", out=ntau[:], in0=ntau[:], scalar1=-1.0, scalar2=None,
        op0=ALU.mult)
    p.barrier()

    for hd in range(NH8):
        p.wait("sync", "tensor")
        p.wait("sync", "scalar")
        p.dma("dsync", qh[:], qn[hd * 128:(hd + 1) * 128, :])
        p.dma("dsync", kh[:], kn[hd * 128:(hd + 1) * 128, :])
        p.wait("tensor", "dsync")
        for qb in range(NQB):
            qsl = slice(qb * 512, (qb + 1) * 512)
            nkt = (qb + 1) * 4
            p.wait("tensor", "vector")
            for kt in range(nkt):
                p.wait("tensor", "scalar")
                p.T("matmul", ps_s[:], kh[:, kt * 128:(kt + 1) * 128], qh[:, qsl],
                    start=True, stop=True)
                p.wait("scalar", "tensor")
                p.wait("scalar", "vector")
                p.S("activation", out=pt[:], in_=ps_s[:], func=AF.Exp,
                    scale=tau[:, 0:1], bias=ntau[:, 0:1])
                diag = kt - 4 * qb
                if diag >= 0:
                    p.wait("vector", "scalar")
                    p.V("tensor_tensor", out=pt[:], in0=pt[:],
                        in1=cm[:, diag * 512:(diag + 1) * 512], op=ALU.mult)
                    p.wait("tensor", "vector")
                else:
                    p.wait("tensor", "scalar")
                p.T("matmul", ps_n[:], vh[kt][:, hd * 128:(hd + 1) * 128], pt[:],
                    start=(kt == 0), stop=(kt == nkt - 1))
                p.T("matmul", ps_d[:], onesb[:], pt[:],
                    start=(kt == 0), stop=(kt == nkt - 1))
            p.wait("vector", "tensor")
            p.V("reciprocal", out=dnr[:], in_=ps_d[:])
            p.wait("tensor", "vector")
            p.T("matmul", ps_b[:], onesm[:], dnr[:], start=True, stop=True)
            p.wait("vector", "tensor")
            p.V("tensor_copy", out=nev[:], in_=ps_n[:])
            p.wait("vector", "dsync")
            p.V("tensor_tensor", out=oev[:], in0=nev[:], in1=ps_b[:], op=ALU.mult)
            p.wait("sync", "vector")
            p.dma("dsync", attno[hd * 128:(hd + 1) * 128, qsl], oev[:])
    p.wait("sync", "dsync")


_ones_m_cache = {}


def p_ones_m(p):
    if id(p) not in _ones_m_cache:
        t = p.sbuf("ones_m2", [1, 128], F32)
        p.G("memset", t[:], 1.0)
        p.barrier()
        _ones_m_cache[id(p)] = t
    return _ones_m_cache[id(p)][:]



def emit_l0b_post(p):
    att = p.ein("att", [D, TC])
    nuv = p.ein("nuv", [128, 1])
    xf = p.ein("xf", [D, TC])
    xo = p.eout("xo", [D, TC])
    NI = D // 128
    nu = p.sbuf("nu", [128, 1], F32)
    ab = [p.sbuf("ab", [128, TC], F32) for _ in range(2)]
    xb = [p.sbuf("xb", [128, TC], F32) for _ in range(2)]
    ob = [p.sbuf("ob", [128, TC], F32) for _ in range(2)]
    p.dma("dsync", nu[:], nuv)
    p.barrier()
    for ci in range(NI):
        i = ci % 2
        p.wait("sync", "vector")
        p.dma("dsync", ab[i][:], att[ci * 128:(ci + 1) * 128, :])
        p.dma("dsync", xb[i][:], xf[ci * 128:(ci + 1) * 128, :])
        p.wait("vector", "dsync")
        p.V("scalar_tensor_tensor", out=ob[i][:], in0=ab[i][:], scalar=nu[:, 0:1],
            in1=xb[i][:], op0=ALU.mult, op1=ALU.add)
        p.wait("sync", "vector")
        p.dma("dsync", xo[ci * 128:(ci + 1) * 128, :], ob[i][:])
    p.wait("sync", "dsync")


M2_NH = 64
M2_HD = 64
M2_NS = 128
M2_IN = 4096
M2_CONV = 4352
M2_PROJ = 8512
M2_CH = 256
M2_NCH = T // M2_CH



def emit_m2_pre1(p):
    xf = p.ein("xf", [D, TC])
    wip = p.ein("wip", [D, M2_PROJ])
    consts = p.ein("consts", [128, C_NCONST])
    gate = p.eout("gate", [M2_IN, TC])
    hsraw = p.eout("hsraw", [M2_CONV, TC])
    dtr2 = p.eout("dtr2", [64, TC])

    NT = TC // 512
    NI = D // 128

    h = make_helpers(p)
    init_helpers(p, h, consts)
    arena = p.sbuf("arena", [128, 8192], F32)
    xc = [arena[:, ci * 512:(ci + 1) * 512] for ci in range(NI)]
    ln = [p.sbuf("ln", [128, TC], BF16) for _ in range(NI)]
    wcol = [p.sbuf("wcol", [128, 16 * 128], BF16) for _ in range(2)]
    ps = [p.psum("ps_mm", [128, 512], F32) for _ in range(2)]
    ob = [p.sbuf("ob", [128, 512], F32) for _ in range(2)]
    p.barrier()

    for tb in range(NT):
        for ci in range(NI):
            p.dma("dsync", xc[ci], xf[ci * 128:(ci + 1) * 128, tb * 512:(tb + 1) * 512])
        p.wait("tensor", "dsync")
        p.wait("scalar", "dsync")
        p.wait("vector", "dsync")
        emit_ln(p, h, [xc[ci] for ci in range(NI)],
                [ln[ci][:, tb * 512:(tb + 1) * 512] for ci in range(NI)], NI, 512)
        p.barrier()

    echunks = [(i * 128, 128) for i in range(66)] + [(8448, 64)]
    for eci, (e0, esz) in enumerate(echunks):
        buf = wcol[eci % 2]
        if eci >= 2:
            p.wait("gpsimd", "tensor")
        src = wip[:, e0:e0 + esz].rearrange("(ic ip) o -> ip ic o", ip=128)
        p.dma("dgps", buf[:, 0:16 * esz].rearrange("p (ic o) -> p ic o", o=esz), src)
        p.wait("tensor", "dgps")
        for tb in range(NT):
            psb, obb = ps[tb % 2], ob[tb % 2]
            if eci * NT + tb >= 2:
                p.wait("tensor", "vector")
            for ic in range(NI):
                p.T("matmul", psb[0:esz, :], buf[:, ic * esz:(ic + 1) * esz],
                    ln[ic][:, tb * 512:(tb + 1) * 512],
                    start=(ic == 0), stop=(ic == NI - 1))
            p.wait("vector", "tensor")
            p.wait("vector", "dsync")
            p.V("tensor_copy", out=obb[0:esz, :], in_=psb[0:esz, :])
            p.wait("sync", "vector")
            tsl = slice(tb * 512, (tb + 1) * 512)
            if e0 < M2_IN:
                p.dma("dsync", gate[e0:e0 + esz, tsl], obb[0:esz, :])
            elif e0 < M2_IN + M2_CONV:
                p.dma("dsync", hsraw[e0 - M2_IN:e0 - M2_IN + esz, tsl], obb[0:esz, :])
            else:
                p.dma("dsync", dtr2[0:esz, tsl], obb[0:esz, :])
    p.wait("sync", "dsync")



def emit_m2_pre2(p):
    hsx = p.ein("hsx", [M2_CONV, TC + 3])
    convw2 = p.ein("convw2", [128, 34 * 4])
    convb2 = p.ein("convb2", [128, 34])
    dtr2 = p.ein("dtr2", [64, TC])
    dtbias = p.ein("dtbias", [64, 1])
    consts = p.ein("consts", [128, C_NCONST])
    xs = p.eout("xs", [M2_IN, TC], BF16)
    bm = p.eout("bm", [128, TC], BF16)
    cm = p.eout("cm", [128, TC], BF16)
    dtv2 = p.eout("dtv2", [64, TC])

    NC = M2_CONV // 128

    h = make_helpers(p, ln=False)
    init_helpers(p, h, consts)
    xb = [p.sbuf("xb", [128, TC + 3], F32) for _ in range(2)]
    cwt = p.sbuf("cwt", [128, 4 * NC], F32)
    cbt = p.sbuf("cbt", [128, NC], F32)
    acc = p.sbuf("acc", [128, TC], F32)
    sil = p.sbuf("sil", [128, TC], BF16)
    dtb = p.sbuf("dtb", [64, 1], F32)
    dtin = p.sbuf("dtin", [64, TC], F32)
    dt0 = p.sbuf("dt0", [64, TC], F32)
    dt1 = p.sbuf("dt1", [64, TC], F32)
    dt2 = p.sbuf("dt2", [64, TC], F32)

    p.dma("dsync", cwt[:], convw2)
    p.dma("dsync", cbt[:], convb2)
    p.dma("dsync", dtb[:], dtbias)
    p.dma("dsync", dtin[:], dtr2)
    p.barrier()

    for ci in range(NC):
        buf = xb[ci % 2]
        if ci >= 2:
            p.wait("sync", "vector")
        p.dma("dsync", buf[:], hsx[ci * 128:(ci + 1) * 128, :])
        p.wait("vector", "dsync")
        p.V("tensor_scalar", out=acc[:], in0=buf[:, 0:TC],
            scalar1=cwt[:, 4 * ci:4 * ci + 1], scalar2=cbt[:, ci:ci + 1],
            op0=ALU.mult, op1=ALU.add)
        for k in range(1, 4):
            p.V("scalar_tensor_tensor", out=acc[:], in0=buf[:, k:k + TC],
                scalar=cwt[:, 4 * ci + k:4 * ci + k + 1], in1=acc[:],
                op0=ALU.mult, op1=ALU.add)
        p.wait("scalar", "vector")
        p.wait("scalar", "dsync")
        p.S("activation", out=sil[:], in_=acc[:], func=AF.Silu)
        p.wait("sync", "scalar")
        if ci < 32:
            p.dma("dsync", xs[ci * 128:(ci + 1) * 128, :], sil[:])
        elif ci == 32:
            p.dma("dsync", bm[:, :], sil[:])
        else:
            p.dma("dsync", cm[:, :], sil[:])
    p.V("tensor_scalar", out=dt0[:], in0=dtin[:], scalar1=dtb[:, 0:1],
        scalar2=None, op0=ALU.add)
    emit_softplus(p, h, dt1[:], dt0[:], dt2[:])
    p.wait("sync", "vector")
    p.dma("dsync", dtv2[:, :], dt1[:])
    p.wait("sync", "dsync")



def emit_m2_scan(p):
    NHH = 32
    xs_h = p.ein("xs_h", [NHH * M2_HD, T], BF16)
    bmg = p.ein("bm", [128, T], BF16)
    cmg = p.ein("cm", [128, T], BF16)
    dtv2h = p.ein("dtv2h", [NHH, T])
    alog2 = p.ein("alog2", [1, NHH])
    ltones = p.ein("ltones", [128, 256])
    id128b = p.ein("id128b", [128, 128], BF16)
    id128f = p.ein("id128f", [128, 128])
    cmask256 = p.ein("cmask256", [128, 512], BF16)
    y2 = p.eout("y2", [NHH * M2_HD, T])

    CH = M2_CH
    bsb = p.sbuf("bsb", [128, T], BF16)
    csb = p.sbuf("csb", [128, T], BF16)
    dtsb = p.sbuf("dtsb", [NHH, T], F32)
    arow = p.sbuf("arow", [1, NHH], F32)
    abc = p.sbuf("abc", [128, NHH], F32)
    lt = p.sbuf("lt", [128, 256], F32)
    idb = p.sbuf("idb", [128, 128], BF16)
    idf = p.sbuf("idf", [128, 128], F32)
    cmk = p.sbuf("cmk", [128, 512], BF16)
    onesm = p.sbuf("onesm", [1, 128], F32)
    xsb = [p.sbuf("xsb", [128, 16 * CH], BF16) for _ in range(2)]
    xtok = p.sbuf("xtok", [128, 2 * 2048], BF16)
    btok = p.sbuf("btok", [128, 2 * 128], BF16)
    dtok = p.sbuf("dtok", [128, 2 * NHH], F32)
    adtok = p.sbuf("adtok", [128, 2 * NHH], F32)
    cumt = p.sbuf("cumt", [128, 2 * NHH], F32)
    ncumt = p.sbuf("ncumt", [128, 2 * NHH], F32)
    cumrow = p.sbuf("cumrow", [1, 256], F32)
    sgev = p.sbuf("sgev", [128, 512], F32)
    ee = [p.sbuf("ee", [128, 256], F32) for _ in range(2)]
    ppt = [p.sbuf("ppt", [128, 256], BF16) for _ in range(2)]
    sdecrow = p.sbuf("sdecrow", [128, 256], F32)
    cdec = p.sbuf("cdec", [128, 256], BF16)
    blast = p.sbuf("blast", [128, 1], F32)
    lamc = p.sbuf("lamc", [128, 1], F32)
    deccol = p.sbuf("deccol", [128, 2], F32)
    xdt = p.sbuf("xdt", [128, 2 * 64], BF16)
    xdd = p.sbuf("xdd", [128, 2 * 64], BF16)
    prev = [p.sbuf("prev", [128, M2_HD], F32) for _ in range(NHH)]
    prevb = p.sbuf("prevb", [128, M2_HD], BF16)
    ybuf = p.sbuf("ybuf", [128, 16 * CH], F32)
    psA = p.psum("psA", [128, 512], F32)
    psT = p.psum("psT", [128, 128], BF16)
    psF = p.psum("psF", [128, 256], F32)
    psY = p.psum("psY", [64, 256], F32)
    psS = p.psum("psS", [128, 64], F32)
    psR = p.psum("psR", [128, 256], F32)
    psl = p.psum("psl", [128, 1], F32)

    p.dma("dsync", bsb[:], bmg)
    p.dma("dsync", csb[:], cmg)
    p.dma("dsync", dtsb[:], dtv2h)
    p.dma("dsync", arow[:], alog2)
    p.dma("dsync", lt[:], ltones)
    p.dma("dsync", idb[:], id128b)
    p.dma("dsync", idf[:], id128f)
    p.dma("dsync", cmk[:], cmask256)
    p.G("memset", onesm[:], 1.0)
    for i in range(NHH):
        p.G("memset", prev[i][:], 0.0)
    p.barrier()
    p.S("activation", out=arow[:], in_=arow[:], func=AF.Exp)
    p.wait("vector", "scalar")
    p.V("tensor_scalar", out=arow[:], in0=arow[:], scalar1=-1.0, scalar2=None,
        op0=ALU.mult)
    p.wait("tensor", "vector")
    p.T("matmul", psF[:, 0:NHH], onesm[:], arow[:], start=True, stop=True)
    p.wait("vector", "tensor")
    p.V("tensor_copy", out=abc[:], in_=psF[:, 0:NHH])
    p.barrier()

    for c in range(M2_NCH):
        t0 = c * CH
        tsl = slice(t0, t0 + CH)
        xb = xsb[c % 2]
        if c >= 2:
            p.wait("sync", "tensor")
        p.dma("dsync", xb[:], xs_h[:, tsl].rearrange("(dc pp) t -> pp dc t", pp=128))
        p.wait("tensor", "vector")
        for sh in range(2):
            p.T("matmul", psA[:, sh * 256:(sh + 1) * 256],
                bsb[:, t0 + sh * 128:t0 + sh * 128 + 128], csb[:, tsl],
                start=True, stop=True)
        p.wait("vector", "tensor")
        p.V("tensor_copy", out=sgev[:], in_=psA[:])
        p.wait("tensor", "dsync")
        for sh in range(2):
            for dc in range(16):
                p.wait("tensor", "vector")
                p.T("transpose", psT[:], xb[:, dc * CH + sh * 128:dc * CH + sh * 128 + 128],
                    idb[:])
                p.wait("vector", "tensor")
                p.V("tensor_copy", out=xtok[:, sh * 2048 + dc * 128:sh * 2048 + (dc + 1) * 128],
                    in_=psT[:])
        for sh in range(2):
            p.wait("tensor", "vector")
            p.T("transpose", psT[:], bsb[:, t0 + sh * 128:t0 + sh * 128 + 128], idb[:])
            p.wait("vector", "tensor")
            p.V("tensor_copy", out=btok[:, sh * 128:(sh + 1) * 128], in_=psT[:])
        for sh in range(2):
            p.wait("tensor", "vector")
            p.T("transpose", psF[:, 0:NHH],
                dtsb[:, t0 + sh * 128:t0 + sh * 128 + 128], idf[0:NHH, 0:NHH])
            p.wait("vector", "tensor")
            p.V("tensor_copy", out=dtok[:, sh * NHH:(sh + 1) * NHH], in_=psF[:, 0:NHH])
        p.V("tensor_tensor", out=adtok[:, 0:NHH], in0=dtok[:, 0:NHH], in1=abc[:],
            op=ALU.mult)
        p.V("tensor_tensor", out=adtok[:, NHH:2 * NHH], in0=dtok[:, NHH:2 * NHH],
            in1=abc[:], op=ALU.mult)
        p.wait("tensor", "vector")
        p.T("matmul", psF[:, 0:NHH], lt[:, 0:128], adtok[:, 0:NHH],
            start=True, stop=True)
        p.T("matmul", psF[:, NHH:2 * NHH], lt[:, 128:256], adtok[:, 0:NHH],
            start=True, stop=False)
        p.T("matmul", psF[:, NHH:2 * NHH], lt[:, 0:128], adtok[:, NHH:2 * NHH],
            start=False, stop=True)
        p.wait("vector", "tensor")
        p.V("tensor_copy", out=cumt[:], in_=psF[:, 0:2 * NHH])
        p.V("tensor_scalar", out=ncumt[:], in0=cumt[:], scalar1=-1.0, scalar2=None,
            op0=ALU.mult)

        for hd in range(NHH):
            for sh in range(2):
                p.wait("tensor", "vector")
                p.T("transpose", psF[:, 128:256][0:1, 0:128],
                    cumt[:, sh * NHH + hd:sh * NHH + hd + 1], idf[:])
                p.wait("vector", "tensor")
                p.V("tensor_copy", out=cumrow[0:1, sh * 128:(sh + 1) * 128],
                    in_=psF[0:1, 128:256])
            p.wait("tensor", "vector")
            p.wait("tensor", "scalar")
            p.T("matmul", psR[:], onesm[:], cumrow[:], start=True, stop=True)
            p.T("matmul", psl[:], onesm[:], cumrow[0:1, 255:256], start=True, stop=True)
            p.wait("vector", "tensor")
            p.wait("vector", "scalar")
            p.V("tensor_copy", out=blast[:], in_=psl[:])
            p.wait("scalar", "vector")
            p.wait("scalar", "tensor")
            p.S("activation", out=sdecrow[:], in_=psR[:], func=AF.Exp)
            p.wait("vector", "scalar")
            p.V("tensor_scalar", out=ee[0][:], in0=psR[:], scalar1=ncumt[:, hd:hd + 1],
                scalar2=0.0, op0=ALU.add, op1=ALU.min)
            p.V("tensor_scalar", out=ee[1][:], in0=psR[:],
                scalar1=ncumt[:, NHH + hd:NHH + hd + 1], scalar2=0.0,
                op0=ALU.add, op1=ALU.min)
            p.wait("scalar", "vector")
            p.S("activation", out=ee[0][:], in_=ee[0][:], func=AF.Exp)
            p.S("activation", out=ee[1][:], in_=ee[1][:], func=AF.Exp)
            p.S("activation", out=lamc[:], in_=blast[:], func=AF.Exp)
            p.S("activation", out=deccol[:, 0:1], in_=cumt[:, hd:hd + 1], func=AF.Exp,
                scale=-1.0, bias=blast[:, 0:1])
            p.S("activation", out=deccol[:, 1:2], in_=cumt[:, NHH + hd:NHH + hd + 1],
                func=AF.Exp, scale=-1.0, bias=blast[:, 0:1])
            p.wait("vector", "scalar")
            for sh in range(2):
                p.V("tensor_tensor", out=ee[sh][:], in0=sgev[:, sh * 256:(sh + 1) * 256],
                    in1=ee[sh][:], op=ALU.mult)
                p.V("tensor_tensor", out=ppt[sh][:], in0=ee[sh][:],
                    in1=cmk[:, sh * 256:(sh + 1) * 256], op=ALU.mult)
            for sh in range(2):
                p.V("tensor_scalar", out=xdt[:, sh * 64:(sh + 1) * 64],
                    in0=xtok[:, sh * 2048 + hd * 64:sh * 2048 + (hd + 1) * 64],
                    scalar1=dtok[:, sh * NHH + hd:sh * NHH + hd + 1], scalar2=None,
                    op0=ALU.mult)
                p.V("tensor_scalar", out=xdd[:, sh * 64:(sh + 1) * 64],
                    in0=xdt[:, sh * 64:(sh + 1) * 64],
                    scalar1=deccol[:, sh:sh + 1], scalar2=None, op0=ALU.mult)
            p.V("tensor_tensor", out=cdec[:], in0=csb[:, tsl], in1=sdecrow[:],
                op=ALU.mult)
            p.V("tensor_copy", out=prevb[:], in_=prev[hd][:])
            p.wait("tensor", "vector")
            p.wait("tensor", "scalar")
            for sh in range(2):
                p.T("matmul", psY[:], xdt[:, sh * 64:(sh + 1) * 64], ppt[sh][:],
                    start=(sh == 0), stop=False)
            p.T("matmul", psY[:], prevb[:], cdec[:], start=False, stop=True)
            for sh in range(2):
                p.T("matmul", psS[:], btok[:, sh * 128:(sh + 1) * 128],
                    xdd[:, sh * 64:(sh + 1) * 64], start=(sh == 0), stop=(sh == 1))
            p.wait("vector", "tensor")
            p.V("scalar_tensor_tensor", out=prev[hd][:], in0=prev[hd][:],
                scalar=lamc[:, 0:1], in1=psS[:], op0=ALU.mult, op1=ALU.add)
            p.V("tensor_copy",
                out=ybuf[(hd % 2) * 64:(hd % 2) * 64 + 64, (hd // 2) * CH:(hd // 2 + 1) * CH],
                in_=psY[:])
        p.wait("sync", "vector")
        dst = y2[:, tsl].rearrange("(hp h2 d) t -> h2 d hp t", h2=2, d=64)
        p.dma("dsync", dst, ybuf[:].rearrange("p (hp t) -> p hp t", t=CH))
    p.wait("sync", "dsync")





def emit_m2_post(p):
    y2t = p.ein("y2t", [M2_IN, TC])
    xst = p.ein("xst", [M2_IN, TC], BF16)
    gate = p.ein("gate", [M2_IN, TC])
    drep = p.ein("drep", [128, 32])
    nrmw = p.ein("nrmw", [128, 32])
    wop2 = p.ein("wop2", [M2_IN, D])
    xf = p.ein("xf", [D, TC])
    consts = p.ein("consts", [128, C_NCONST])
    xo = p.eout("xo", [D, TC])
    ygs = p.nc.dram_tensor("ygs", [M2_IN, TC], BF16).ap()

    NK = M2_IN // 128
    NM = D // 128
    NT = TC // 512

    h = make_helpers(p, ln=False)
    init_helpers(p, h, consts)
    dsb = p.sbuf("dsb", [128, 32], F32)
    nwb = p.sbuf("nwb", [128, 32], F32)
    ya = [p.sbuf("ya", [128, 512], F32) for _ in range(2)]
    xab = [p.sbuf("xab", [128, 512], BF16) for _ in range(2)]
    zb = [p.sbuf("zb", [128, 512], F32) for _ in range(2)]
    sz = p.sbuf("sz", [128, 512], F32)
    sq = p.sbuf("sq", [128, 512], F32)
    ygo = [p.sbuf("ygo", [128, 512], BF16) for _ in range(2)]
    ps_ss = [p.psum("ps_ss", [1, 512], F32) for _ in range(NT)]
    ssrow = p.sbuf("ssrow", [1, TC], F32)
    ps_bc = p.psum("ps_bc", [128, 512], F32)
    rstd = p.sbuf("rstd", [128, TC], F32)
    ygb = [p.sbuf("ygb", [128, TC], BF16) for _ in range(NK)]
    wcol = p.sbuf("wcol3", [128, NK * 128], BF16)
    ps = [p.psum("ps_o2", [128, 512], F32) for _ in range(2)]
    xfb = [p.sbuf("xfb", [128, 512], F32) for _ in range(2)]
    ob = [p.sbuf("ob3", [128, 512], F32) for _ in range(2)]

    p.dma("dsync", dsb[:], drep)
    p.dma("dsync", nwb[:], nrmw)
    p.barrier()

    for kc in range(NK):
        for tb in range(NT):
            i = (kc * NT + tb) % 2
            sl = slice(tb * 512, (tb + 1) * 512)
            rows = slice(kc * 128, (kc + 1) * 128)
            p.wait("sync", "vector")
            p.wait("gpsimd", "vector")
            p.dma("dsync", ya[i][:], y2t[rows, sl])
            p.dma("dgps", xab[i][:], xst[rows, sl])
            p.dma("dscal", zb[i][:], gate[rows, sl])
            p.wait("vector", "dsync")
            p.wait("vector", "dgps")
            p.V("scalar_tensor_tensor", out=ya[i][:], in0=xab[i][:],
                scalar=dsb[:, kc:kc + 1], in1=ya[i][:], op0=ALU.mult, op1=ALU.add)
            p.wait("scalar", "dscal")
            p.wait("scalar", "vector")
            p.S("activation", out=sz[:], in_=zb[i][:], func=AF.Silu)
            p.wait("vector", "scalar")
            p.wait("vector", "dsync")
            p.V("tensor_tensor", out=ya[i][:], in0=ya[i][:], in1=sz[:], op=ALU.mult)
            p.V("tensor_copy", out=ygo[i][:], in_=ya[i][:])
            p.wait("sync", "vector")
            p.dma("dsync", ygs[rows, sl], ygo[i][:])
            p.wait("scalar", "vector")
            p.wait("scalar", "tensor")
            p.S("activation", out=sq[:], in_=ya[i][:], func=AF.Square)
            p.wait("tensor", "scalar")
            p.T("matmul", ps_ss[tb][:], h["ones_k"][:], sq[:],
                start=(kc == 0), stop=(kc == NK - 1))
    p.wait("scalar", "tensor")
    for tb in range(NT):
        p.S("activation", out=ssrow[0:1, tb * 512:(tb + 1) * 512],
            in_=ps_ss[tb][:], func=AF.Sqrt, scale=1.0 / M2_IN,
            bias=h["consts"][0:1, C_EPS_LN:C_EPS_LN + 1])
    p.wait("vector", "scalar")
    p.V("reciprocal", out=ssrow[:], in_=ssrow[:])
    for tb in range(NT):
        p.wait("tensor", "vector")
        p.T("matmul", ps_bc[:], h["ones_m"][:], ssrow[0:1, tb * 512:(tb + 1) * 512],
            start=True, stop=True)
        p.wait("vector", "tensor")
        p.V("tensor_copy", out=rstd[:, tb * 512:(tb + 1) * 512], in_=ps_bc[:])
    p.barrier()
    for kc in range(NK):
        p.wait("sync", "vector")
        p.dma("dsync", ygb[kc][:], ygs[kc * 128:(kc + 1) * 128, :])
    p.barrier()
    for kc in range(NK):
        p.V("tensor_tensor", out=ygb[kc][:], in0=ygb[kc][:], in1=rstd[:], op=ALU.mult)
        p.V("tensor_scalar", out=ygb[kc][:], in0=ygb[kc][:], scalar1=nwb[:, kc:kc + 1],
            scalar2=None, op0=ALU.mult)
    p.barrier()
    for mc in range(NM):
        src2 = wop2[:, mc * 128:(mc + 1) * 128].rearrange("(kc kp) m -> kp kc m", kp=128)
        p.wait("gpsimd", "tensor")
        p.dma("dgps", wcol[:].rearrange("pp (kc m) -> pp kc m", m=128), src2)
        p.wait("tensor", "dgps")
        for tb in range(NT):
            pp = ps[tb % 2]
            sl = slice(tb * 512, (tb + 1) * 512)
            if mc * NT + tb >= 2:
                p.wait("tensor", "vector")
            for kc in range(NK):
                p.T("matmul", pp[:], wcol[:, kc * 128:(kc + 1) * 128],
                    ygb[kc][:, sl], start=(kc == 0), stop=(kc == NK - 1))
            i = tb % 2
            p.wait("sync", "vector")
            p.dma("dsync", xfb[i][:], xf[mc * 128:(mc + 1) * 128, sl])
            p.wait("vector", "dsync")
            p.wait("vector", "tensor")
            p.V("tensor_tensor", out=ob[i][:], in0=pp[:], in1=xfb[i][:], op=ALU.add)
            p.wait("sync", "vector")
            p.dma("dsync", xo[mc * 128:(mc + 1) * 128, sl], ob[i][:])
        p.barrier()
    p.wait("sync", "dsync")



def emit_w2prep(p):
    w2T = p.ein("w2T", [8192, 256])
    w2p = p.eout("w2p", [8192, 256], BF16)
    ch = [p.sbuf("ch", [128, 256], F32) for _ in range(64)]
    tmp = p.sbuf("tmp", [128, 256], F32)
    outb = p.sbuf("outb", [128, 256], BF16)
    for a in range(64):
        p.dma("dsync", ch[a][:], w2T[a * 128:(a + 1) * 128, :])
    p.barrier()
    for s in range(6):
        step = 1 << s
        for a in range(64):
            if a & step:
                continue
            b = a | step
            p.V("tensor_tensor", out=tmp[:], in0=ch[a][:], in1=ch[b][:], op=ALU.subtract)
            p.V("tensor_tensor", out=ch[a][:], in0=ch[a][:], in1=ch[b][:], op=ALU.add)
            p.V("tensor_copy", out=ch[b][:], in_=tmp[:])
    sc = 1.0 / float(np.sqrt(8192.0))
    for a in range(64):
        p.wait("vector", "dsync")
        p.V("tensor_scalar", out=outb[:], in0=ch[a][:], scalar1=sc, scalar2=None,
            op0=ALU.mult)
        p.wait("sync", "vector")
        p.dma("dsync", w2p[a * 128:(a + 1) * 128, :], outb[:])
    p.wait("sync", "dsync")



def emit_l1ffn(p):
    xf = p.ein("xf", [D, TC])
    w1T = p.ein("w1T", [D, 8192])
    w3T = p.ein("w3T", [D, 8192])
    w2p = p.ein("w2p", [8192, D], BF16)
    h128 = p.ein("h128", [128, 128], BF16)
    id128f = p.ein("id128f", [128, 128])
    consts = p.ein("consts", [128, C_NCONST])
    xo = p.eout("xo", [D, TC])
    z1s = p.nc.dram_tensor("z1s", [8192, TC], BF16).ap()

    NI = D // 128
    NT = TC // 512
    NH = 8192 // 128

    pool = [p.psum(f"bank{i}", [128, 512], F32) for i in range(8)]
    h = make_helpers(p, pool=pool)
    init_helpers(p, h, consts)
    arena = p.sbuf("arena", [128, 8192], F32)
    xc = [arena[:, ci * 512:(ci + 1) * 512] for ci in range(NI)]
    ln = [p.sbuf("ln", [128, TC], BF16) for _ in range(NI)]
    hsb = p.sbuf("hsb", [128, 128], BF16)
    idf = p.sbuf("idf", [128, 128], F32)
    wc1 = [p.sbuf("wc1", [128, 16 * 128], BF16) for _ in range(2)]
    wc3 = [p.sbuf("wc3", [128, 16 * 128], BF16) for _ in range(2)]
    psU = [pool[4][:], pool[5][:]]
    psV_ = [pool[6][:], pool[7][:]]
    su = p.sbuf("su", [128, 512], F32)
    hb = p.sbuf("hb", [128, 512], BF16)
    psZ = pool[0][:]
    z1b = [p.sbuf("z1b", [128, 2048], BF16) for _ in range(1)]
    p.dma("dsync", hsb[:], h128)
    p.dma("dsync", idf[:], id128f)
    p.barrier()

    for tb in range(NT):
        for ci in range(NI):
            p.dma("dsync", xc[ci], xf[ci * 128:(ci + 1) * 128, tb * 512:(tb + 1) * 512])
        p.wait("tensor", "dsync")
        p.wait("scalar", "dsync")
        p.wait("vector", "dsync")
        emit_ln(p, h, [xc[ci] for ci in range(NI)],
                [ln[ci][:, tb * 512:(tb + 1) * 512] for ci in range(NI)], NI, 512)
        p.barrier()

    def w1src(hc):
        return w1T[:, hc * 128:(hc + 1) * 128].rearrange("(ic ip) o -> ip ic o", ip=128)

    def w3src(hc):
        return w3T[:, hc * 128:(hc + 1) * 128].rearrange("(ic ip) o -> ip ic o", ip=128)

    ldc = {}
    tdone = {}
    sUd = {}
    hbv = {}
    zev = {}
    su2 = [p.sbuf("su2", [128, 512], F32) for _ in range(2)]
    hb2 = [p.sbuf("hb2", [128, 512], BF16) for _ in range(2)]
    psZ2 = [pool[0][:], pool[1][:]]
    p.dma("dgps", wc1[0][:].rearrange("pp (ic o) -> pp ic o", o=128), w1src(0))
    p.dma("dgps", wc3[0][:].rearrange("pp (ic o) -> pp ic o", o=128), w3src(0))
    ldc[0] = p.cnt["dgps"]
    for hc in range(NH):
        b1, b3 = wc1[hc % 2], wc3[hc % 2]
        if hc + 1 < NH:
            if hc >= 1:
                p.wait("gpsimd", "tensor", at=tdone.get(hc - 1))
            p.dma("dgps", wc1[(hc + 1) % 2][:].rearrange("pp (ic o) -> pp ic o", o=128),
                  w1src(hc + 1))
            p.dma("dgps", wc3[(hc + 1) % 2][:].rearrange("pp (ic o) -> pp ic o", o=128),
                  w3src(hc + 1))
            ldc[hc + 1] = p.cnt["dgps"]
        p.wait("tensor", "dgps", at=ldc[hc])
        for tb in range(NT):
            i = tb % 2
            pu, pv = psU[i], psV_[i]
            tsl = slice(tb * 512, (tb + 1) * 512)
            if sUd.get(i):
                p.wait("tensor", "scalar", at=sUd[i])
            if hbv.get(i):
                p.wait("tensor", "vector", at=hbv[i])
            for ic in range(NI):
                p.T("matmul", pu, b1[:, ic * 128:(ic + 1) * 128], ln[ic][:, tsl],
                    start=(ic == 0), stop=(ic == NI - 1))
            tU = p.cnt["tensor"]
            for ic in range(NI):
                p.T("matmul", pv, b3[:, ic * 128:(ic + 1) * 128], ln[ic][:, tsl],
                    start=(ic == 0), stop=(ic == NI - 1))
            tV = p.cnt["tensor"]
            p.wait("scalar", "tensor", at=tU)
            p.S("activation", out=su2[i][:], in_=pu, func=AF.Silu)
            sUd[i] = p.cnt["scalar"]
            p.wait("vector", "scalar", at=sUd[i])
            p.wait("vector", "tensor", at=tV)
            p.V("tensor_tensor", out=hb2[i][:], in0=su2[i][:], in1=pv, op=ALU.mult)
            hbv[i] = p.cnt["vector"]
            p.wait("tensor", "vector", at=hbv[i])
            if zev.get(i):
                p.wait("tensor", "vector", at=zev[i])
            p.T("matmul", psZ2[i], hsb[:], hb2[i][:], start=True, stop=True)
            tz = p.cnt["tensor"]
            p.wait("vector", "tensor", at=tz)
            p.V("tensor_copy", out=z1b[0][:, tsl], in_=psZ2[i])
            zev[i] = p.cnt["vector"]
        tdone[hc] = p.cnt["tensor"]
        p.wait("sync", "vector", at=zev[(NT - 1) % 2])
        p.dma("dsync", z1s[hc * 128:(hc + 1) * 128, :], z1b[0][:])
        p.wait("vector", "dsync")
    p.barrier()

    z1c = [p.sbuf("z1c", [128, 64 * 128], BF16) for _ in range(2)]
    w2b = [p.sbuf("w2b", [128, 2048], BF16) for _ in range(2)]
    psO = [pool[i][:] for i in range(4)]
    psX = pool[4][:, 0:128]
    otb = p.sbuf("otb", [128, 16 * 128], F32)
    xfb = [p.sbuf("xfb2", [128, 16 * 128], F32) for _ in range(1)]
    zldc = {}
    wldc = {}
    wtd = {}
    p.dma("dsync", z1c[0][:].rearrange("pp (hc t) -> pp hc t", t=128),
          z1s[:, 0:128].rearrange("(hc ip) t -> ip hc t", ip=128))
    zldc[0] = p.cnt["dsync"]
    p.dma("dgps", w2b[0][:], w2p[0:128, :])
    wldc[0] = p.cnt["dgps"]
    NTT = TC // 128
    otv = 0
    for tt in range(NTT):
        zc = z1c[tt % 2]
        if tt + 1 < NTT:
            p.wait("sync", "tensor", at=wtd.get((tt - 1, NH - 1)))
            p.dma("dsync", z1c[(tt + 1) % 2][:].rearrange("pp (hc t) -> pp hc t", t=128),
                  z1s[:, (tt + 1) * 128:(tt + 2) * 128].rearrange(
                      "(hc ip) t -> ip hc t", ip=128))
            zldc[tt + 1] = p.cnt["dsync"]
        if otv:
            p.wait("scalar", "vector", at=otv)
        p.dma("dscal", xfb[0][:].rearrange("pp (mc t) -> pp mc t", t=128),
              xf[:, tt * 128:(tt + 1) * 128].rearrange("(mc ip) t -> ip mc t", ip=128))
        xfl = p.cnt["dscal"]
        p.wait("tensor", "dsync", at=zldc[tt])
        for hc in range(NH):
            wb = w2b[hc % 2]
            nhc = tt * NH + hc
            if nhc + 1 < NTT * NH:
                nxt = (hc + 1) % NH
                p.wait("gpsimd", "tensor", at=wtd.get((tt, hc - 1)) if hc >= 1
                       else wtd.get((tt - 1, NH - 1)))
                p.dma("dgps", w2b[(hc + 1) % 2][:], w2p[nxt * 128:(nxt + 1) * 128, :])
                wldc[nhc + 1] = p.cnt["dgps"]
            p.wait("tensor", "dgps", at=wldc[nhc])
            for mg in range(4):
                p.T("matmul", psO[mg], zc[:, hc * 128:(hc + 1) * 128],
                    wb[:, mg * 512:(mg + 1) * 512],
                    start=(hc == 0), stop=(hc == NH - 1))
            wtd[(tt, hc)] = p.cnt["tensor"]
        p.wait("vector", "dsync")
        for mc in range(16):
            mg, mi = mc // 4, mc % 4
            p.wait("vector", "tensor")
            p.V("tensor_copy", out=otb[:, mc * 128:(mc + 1) * 128],
                in_=psO[mg][:, mi * 128:(mi + 1) * 128])
        for mc in range(16):
            p.wait("tensor", "vector")
            p.T("transpose", psX, otb[:, mc * 128:(mc + 1) * 128], idf[:])
            p.wait("vector", "tensor")
            p.wait("vector", "dscal", at=xfl)
            p.V("tensor_tensor", out=otb[:, mc * 128:(mc + 1) * 128], in0=psX,
                in1=xfb[0][:, mc * 128:(mc + 1) * 128], op=ALU.add)
        otv = p.cnt["vector"]
        p.wait("sync", "vector", at=otv)
        p.dma("dsync", xo[:, tt * 128:(tt + 1) * 128].rearrange(
            "(mc ip) t -> ip mc t", ip=128),
            otb[:].rearrange("pp (mc t) -> pp mc t", t=128))
    p.wait("sync", "dsync")



def emit_l1b_pre(p):
    xf = p.ein("xf", [D, TC])
    expT = p.ein("expT", [D, 8704])
    gamt = p.ein("gamt", [128, 16])
    bett = p.ein("bett", [128, 16])
    consts = p.ein("consts", [128, C_NCONST])
    qg = p.eout("qg", [256, TC], BF16)
    kg = p.eout("kg", [256, TC], BF16)
    gl = p.eout("gl", [D, TC], BF16)
    vt = p.eout("vt", [TC, D], BF16)
    hsv = p.eout("hsv", [D, TC])

    NI = D // 128
    NT = TC // 512

    pool = [p.psum(f"bank{i}", [128, 512], F32) for i in range(8)]
    h = make_helpers(p, pool=pool)
    init_helpers(p, h, consts)
    gam = p.sbuf("gam", [128, 16], F32)
    bet = p.sbuf("bet", [128, 16], F32)
    arena = p.sbuf("arena", [128, 8192], F32)
    xc = [arena[:, ci * 512:(ci + 1) * 512] for ci in range(NI)]
    ln = [p.sbuf("ln", [128, TC], BF16) for _ in range(NI)]
    wcol = [p.sbuf("wcol", [128, 16 * 128], BF16) for _ in range(2)]
    psA_ = [pool[4][:], pool[5][:]]
    geb = p.sbuf("geb", [128, TC], F32)
    glo = p.sbuf("glo", [128, TC], BF16)
    p.dma("dsync", gam[:], gamt)
    p.dma("dsync", bet[:], bett)
    p.barrier()

    hpl = [p.sbuf("hpl", [128, 512], F32) for _ in range(NI)]
    for tb in range(NT):
        for ci in range(NI):
            p.dma("dsync", xc[ci], xf[ci * 128:(ci + 1) * 128, tb * 512:(tb + 1) * 512])
        p.wait("tensor", "dsync")
        p.wait("scalar", "dsync")
        p.wait("vector", "dsync")
        emit_ln(p, h, [xc[ci] for ci in range(NI)],
                [ln[ci][:, tb * 512:(tb + 1) * 512] for ci in range(NI)], NI, 512,
                gamma=[gam[:, ci:ci + 1] for ci in range(NI)],
                beta=[bet[:, ci:ci + 1] for ci in range(NI)],
                out_plain=[hpl[ci][:] for ci in range(NI)])
        p.wait("sync", "vector")
        for ci in range(NI):
            p.dma("dsync", hsv[ci * 128:(ci + 1) * 128, tb * 512:(tb + 1) * 512],
                  hpl[ci][:])
        p.barrier()

    def fm_chunk(e0, out_ap, orow, dtype_note=None):
        eci = e0 // 128
        buf = wcol[eci % 2]
        p.wait("gpsimd", "tensor")
        p.dma("dgps", buf[:].rearrange("pp (ic o) -> pp ic o", o=128),
              expT[:, e0:e0 + 128].rearrange("(ic ip) o -> ip ic o", ip=128))
        p.wait("tensor", "dgps")
        for tb in range(NT):
            pp = psA_[tb % 2]
            if tb >= 2:
                p.wait("tensor", "vector")
            for ic in range(NI):
                p.T("matmul", pp, buf[:, ic * 128:(ic + 1) * 128],
                    ln[ic][:, tb * 512:(tb + 1) * 512],
                    start=(ic == 0), stop=(ic == NI - 1))
            p.wait("vector", "tensor")
            p.wait("vector", "dsync")
            p.V("tensor_copy", out=glo[:, tb * 512:(tb + 1) * 512], in_=pp)
        p.wait("sync", "vector")
        p.dma("dsync", out_ap[orow:orow + 128, :], glo[:])

    for i in range(2):
        fm_chunk(i * 128, qg, i * 128)
    for i in range(2):
        fm_chunk(256 + i * 128, kg, i * 128)
    for i in range(16):
        e0p = 4608 + i * 128
        eci = e0p // 128
        buf = wcol[eci % 2]
        p.wait("gpsimd", "tensor")
        p.dma("dgps", buf[:].rearrange("pp (ic o) -> pp ic o", o=128),
              expT[:, e0p:e0p + 128].rearrange("(ic ip) o -> ip ic o", ip=128))
        p.wait("tensor", "dgps")
        for tb in range(NT):
            pp = psA_[tb % 2]
            if tb >= 2 or i > 0:
                p.wait("tensor", "scalar")
            for ic in range(NI):
                p.T("matmul", pp, buf[:, ic * 128:(ic + 1) * 128],
                    ln[ic][:, tb * 512:(tb + 1) * 512],
                    start=(ic == 0), stop=(ic == NI - 1))
            p.wait("scalar", "tensor")
            p.wait("scalar", "vector")
            p.S("activation", out=geb[:, tb * 512:(tb + 1) * 512], in_=pp, func=AF.Gelu)
        e0l = 512 + i * 128
        buf = wcol[(e0l // 128) % 2]
        p.wait("gpsimd", "tensor")
        p.wait("gpsimd", "scalar")
        p.dma("dgps", buf[:].rearrange("pp (ic o) -> pp ic o", o=128),
              expT[:, e0l:e0l + 128].rearrange("(ic ip) o -> ip ic o", ip=128))
        p.wait("tensor", "dgps")
        for tb in range(NT):
            pp = psA_[tb % 2]
            if True:
                p.wait("tensor", "vector")
                p.wait("tensor", "scalar")
            for ic in range(NI):
                p.T("matmul", pp, buf[:, ic * 128:(ic + 1) * 128],
                    ln[ic][:, tb * 512:(tb + 1) * 512],
                    start=(ic == 0), stop=(ic == NI - 1))
            p.wait("vector", "tensor")
            p.wait("vector", "dsync")
            p.V("tensor_tensor", out=glo[:, tb * 512:(tb + 1) * 512], in0=pp,
                in1=geb[:, tb * 512:(tb + 1) * 512], op=ALU.mult)
        p.wait("sync", "vector")
        p.dma("dsync", gl[i * 128:(i + 1) * 128, :], glo[:])
    p.barrier()

    arena_bf = arena[:].bitcast(BF16)
    vstage = p.sbuf("vstage", [128, 512], BF16)
    gstage = p.sbuf("gstage", [128, 512], F32)
    for og in range(4):
        wres = [arena_bf[:, ic * 512:(ic + 1) * 512] for ic in range(NI)]
        wres2 = [arena_bf[:, 8192 + ic * 512:8192 + (ic + 1) * 512] for ic in range(NI)]
        p.wait("gpsimd", "tensor")
        for ic in range(NI):
            p.dma("dgps", wres[ic],
                  expT[ic * 128:(ic + 1) * 128, 2560 + og * 512:2560 + (og + 1) * 512])
            p.dma("dgps", wres2[ic],
                  expT[ic * 128:(ic + 1) * 128, 6656 + og * 512:6656 + (og + 1) * 512])
        p.wait("tensor", "dgps")
        for tt in range(TC // 128):
            tsl = slice(tt * 128, (tt + 1) * 128)
            ppre = psA_[0]
            plin = psA_[1]
            p.wait("tensor", "scalar")
            p.wait("tensor", "vector")
            for ic in range(NI):
                p.T("matmul", ppre, ln[ic][:, tsl], wres2[ic],
                    start=(ic == 0), stop=(ic == NI - 1))
            p.wait("scalar", "tensor")
            p.S("activation", out=gstage[:], in_=ppre, func=AF.Gelu)
            p.wait("tensor", "scalar")
            for ic in range(NI):
                p.T("matmul", plin, ln[ic][:, tsl], wres[ic],
                    start=(ic == 0), stop=(ic == NI - 1))
            p.wait("vector", "tensor")
            p.wait("vector", "scalar")
            p.wait("vector", "dsync")
            p.V("tensor_tensor", out=vstage[:], in0=plin, in1=gstage[:], op=ALU.mult)
            p.wait("sync", "vector")
            p.dma("dsync", vt[tsl, og * 512:(og + 1) * 512], vstage[:])
    p.wait("sync", "dsync")



def emit_l1b_attn(p):
    qg = p.ein("qg", [256, T], BF16)
    kg = p.ein("kg", [256, T], BF16)
    vth = p.ein("vth", [T, 1024], BF16)
    pmv = p.ein("pmv", [1, 1])
    iotac = p.ein("iotac", [128, 1])
    nmask = p.ein("nmask", [128, 4 * 512])
    id128b = p.ein("id128b", [128, 128], BF16)
    id128f_ = p.ein("id128f", [128, 128])
    consts = p.ein("consts", [128, C_NCONST])
    attn1 = p.eout("attn1", [1024, T])

    NQT = T // 128
    h = make_helpers(p, ln=False)
    init_helpers(p, h, consts)
    pool = [p.psum(f"bank{i}", [128, 512], F32) for i in range(8)]
    qsb = p.sbuf("qsb", [128, 2 * T // 2], BF16)
    qsb2 = p.sbuf("qsb2", [128, T], BF16)
    ksb = p.sbuf("ksb", [128, T], BF16)
    ksb2 = p.sbuf("ksb2", [128, T], BF16)
    vsb = [p.sbuf("vsb", [128, 1024], BF16) for _ in range(T // 128)]
    iot = p.sbuf("iot", [128, 1], F32)
    nmk = p.sbuf("nmk", [128, 4 * 512], F32)
    idb = p.sbuf("idb", [128, 128], BF16)
    idf2 = p.sbuf("idf2", [128, 128], F32)
    spv = p.sbuf("spv", [1, 1], F32)
    spc = p.sbuf("spc", [128, 1], F32)
    nspc = p.sbuf("nspc", [128, 1], F32)
    brow = p.sbuf("brow", [128, 512], F32)
    onesm = p.sbuf("onesm", [1, 128], F32)
    sb = p.sbuf("sb", [128, 512], F32)
    colb = p.sbuf("colb", [128, 1], F32)
    mrow = p.sbuf("mrow", [128, 1], F32)
    mloc = p.sbuf("mloc", [128, 1], F32)
    nm = p.sbuf("nm", [128, 1], F32)
    pbf = p.sbuf("pbf", [128, 512], BF16)
    ptb = p.sbuf("ptb", [128, 512], BF16)
    dacc = p.sbuf("dacc", [128, 32], F32)
    drec = p.sbuf("drec", [128, 1], F32)
    oev = p.sbuf("oev", [128, 1024], F32)
    otb = p.sbuf("otb", [128, 8 * 128], F32)

    p.dma("dsync", qsb[:], qg[0:128, :])
    p.dma("dsync", qsb2[:], qg[128:256, :])
    p.dma("dsync", ksb[:], kg[0:128, :])
    p.dma("dsync", ksb2[:], kg[128:256, :])
    for kt in range(T // 128):
        p.dma("dgps", vsb[kt][:], vth[kt * 128:(kt + 1) * 128, :])
    p.dma("dsync", iot[:], iotac)
    p.dma("dsync", nmk[:], nmask)
    p.dma("dsync", idb[:], id128b)
    p.dma("dsync", idf2[:], id128f_)
    p.dma("dsync", spv[:], pmv)
    p.G("memset", onesm[:], 1.0)
    p.barrier()
    p.V("tensor_scalar", out=qsb[:], in0=qsb[:], scalar1=1.0 / 16.0, scalar2=None,
        op0=ALU.mult)
    p.V("tensor_scalar", out=qsb2[:], in0=qsb2[:], scalar1=1.0 / 16.0, scalar2=None,
        op0=ALU.mult)
    sps = p.sbuf("sps", [1, 1], F32)
    emit_softplus(p, h, sps[:], spv[:], p.sbuf("spscr", [1, 1], F32)[:])
    p.wait("tensor", "vector")
    p.T("matmul", pool[0][:, 0:1], onesm[:], sps[:], start=True, stop=True)
    p.wait("vector", "tensor")
    p.V("tensor_copy", out=spc[:], in_=pool[0][:, 0:1])
    p.V("tensor_scalar", out=nspc[:], in0=spc[:], scalar1=-1.0, scalar2=None,
        op0=ALU.mult)
    ji = p.sbuf("ji", [1, 512], F32)
    p.G("iota", out=ji[:], pattern=[[1, 512]], base=0, channel_multiplier=0,
        allow_small_or_imprecise_dtypes=True)
    p.wait("tensor", "gpsimd")
    p.wait("tensor", "vector")
    p.T("matmul", pool[1][:], onesm[:], ji[:], start=True, stop=True)
    p.wait("vector", "tensor")
    p.V("tensor_scalar", out=brow[:], in0=pool[1][:], scalar1=spc[:, 0:1],
        scalar2=None, op0=ALU.mult)
    p.barrier()

    psS = pool[2][:]
    psN = [pool[3][:], pool[4][:]]
    psT_ = pool[5][:, 0:64].bitcast(BF16)
    psB = pool[6][:]

    for qt in range(NQT):
        q0 = qt * 128
        qsl = slice(q0, q0 + 128)
        nkb = q0 // 512 + 1
        p.wait("vector", "dsync")
        p.V("memset", mrow[:], -1e30)
        for kb in range(nkb):
            k0 = kb * 512
            ksl = slice(k0, k0 + 512)
            p.wait("tensor", "vector")
            p.wait("tensor", "scalar")
            p.T("matmul", psS, qsb[:, qsl], ksb[:, ksl], start=True, stop=False)
            p.T("matmul", psS, qsb2[:, qsl], ksb2[:, ksl], start=False, stop=True)
            p.V("tensor_scalar", out=colb[:], in0=iot[:], scalar1=float(q0 - k0),
                scalar2=None, op0=ALU.add)
            p.V("tensor_tensor", out=colb[:], in0=colb[:], in1=nspc[:], op=ALU.mult)
            p.wait("vector", "tensor")
            p.V("scalar_tensor_tensor", out=sb[:], in0=psS, scalar=colb[:, 0:1],
                in1=brow[:], op0=ALU.add, op1=ALU.add)
            if kb == nkb - 1:
                p.V("tensor_tensor", out=sb[:], in0=sb[:],
                    in1=nmk[:, (qt % 4) * 512:(qt % 4 + 1) * 512], op=ALU.add)
            p.V("tensor_reduce", out=mloc[:], in_=sb[:], axis=AX.X, op=ALU.max)
            p.V("tensor_tensor", out=mrow[:], in0=mrow[:], in1=mloc[:], op=ALU.max)
        p.V("tensor_scalar", out=nm[:], in0=mrow[:], scalar1=-1.0, scalar2=None,
            op0=ALU.mult)
        for kb in range(nkb):
            k0 = kb * 512
            ksl = slice(k0, k0 + 512)
            p.wait("tensor", "vector")
            p.wait("tensor", "scalar")
            p.T("matmul", psS, qsb[:, qsl], ksb[:, ksl], start=True, stop=False)
            p.T("matmul", psS, qsb2[:, qsl], ksb2[:, ksl], start=False, stop=True)
            p.V("tensor_scalar", out=colb[:], in0=iot[:], scalar1=float(q0 - k0),
                scalar2=None, op0=ALU.add)
            p.V("tensor_tensor", out=colb[:], in0=colb[:], in1=nspc[:], op=ALU.mult)
            p.wait("vector", "tensor")
            p.V("scalar_tensor_tensor", out=sb[:], in0=psS, scalar=colb[:, 0:1],
                in1=brow[:], op0=ALU.add, op1=ALU.add)
            if kb == nkb - 1:
                p.V("tensor_tensor", out=sb[:], in0=sb[:],
                    in1=nmk[:, (qt % 4) * 512:(qt % 4 + 1) * 512], op=ALU.add)
            p.wait("scalar", "vector")
            p.wait("scalar", "tensor")
            p.S("activation", out=pbf[:], in_=sb[:], func=AF.Exp,
                bias=nm[:, 0:1], accum_out=dacc[:, kb:kb + 1])
            for kt4 in range(4):
                p.wait("tensor", "scalar")
                p.wait("tensor", "vector")
                p.T("transpose", psT_, pbf[:, kt4 * 128:(kt4 + 1) * 128], idb[:])
                p.wait("vector", "tensor")
                p.V("tensor_copy", out=ptb[:, kt4 * 128:(kt4 + 1) * 128],
                    in_=psT_[:, 0:128])
            kt = kb * 4
            p.wait("tensor", "vector")
            for kt4 in range(4):
                for vg in range(2):
                    p.T("matmul", psN[vg], ptb[:, kt4 * 128:(kt4 + 1) * 128],
                        vsb[kt + kt4][:, vg * 512:(vg + 1) * 512],
                        start=(kb == 0 and kt4 == 0), stop=(kb == nkb - 1 and kt4 == 3))
        p.wait("vector", "scalar")
        p.V("tensor_reduce", out=drec[:], in_=dacc[:, 0:nkb], axis=AX.X, op=ALU.add)
        p.V("reciprocal", out=drec[:], in_=drec[:])
        p.wait("vector", "tensor")
        for vg in range(2):
            p.V("tensor_scalar", out=oev[:, vg * 512:(vg + 1) * 512], in0=psN[vg],
                scalar1=drec[:, 0:1], scalar2=None, op0=ALU.mult)
        for vc in range(8):
            p.wait("tensor", "vector")
            p.T("transpose", pool[7][:, 0:128], oev[:, vc * 128:(vc + 1) * 128],
                idf2[:])
            p.wait("vector", "tensor")
            p.wait("vector", "dsync")
            p.V("tensor_copy", out=otb[:, vc * 128:(vc + 1) * 128],
                in_=pool[7][:, 0:128])
        p.wait("sync", "vector")
        p.dma("dsync", attn1[:, qsl].rearrange("(vc ip) t -> ip vc t", ip=128),
              otb[:].rearrange("pp (vc t) -> pp vc t", t=128))
    p.wait("sync", "dsync")



def emit_l1b_post(p):
    glt = p.ein("glt", [D, TC], BF16)
    at1 = p.ein("at1", [D, TC], BF16)
    hsv = p.ein("hsv", [D, TC])
    xf = p.ein("xf", [D, TC])
    prjT = p.ein("prjT", [2 * D, D])
    xo = p.eout("xo", [D, TC])

    NK = 2 * D // 128
    NM = D // 128
    NT = TC // 512

    fc = [p.sbuf("fc", [128, TC], BF16) for _ in range(NK)]
    wcol = p.sbuf("wcol4", [128, NK * 128], BF16)
    ps = [p.psum("ps_o3", [128, 512], F32) for _ in range(2)]
    xfb = [p.sbuf("xfb3", [128, 512], F32) for _ in range(2)]
    hb = [p.sbuf("hb3", [128, 512], F32) for _ in range(2)]
    ob = [p.sbuf("ob4", [128, 512], F32) for _ in range(2)]

    for kc in range(16):
        p.dma("dsync", fc[kc][:], glt[kc * 128:(kc + 1) * 128, :])
        p.dma("dgps", fc[16 + kc][:], at1[kc * 128:(kc + 1) * 128, :])
    p.barrier()

    for mc in range(NM):
        src2 = prjT[:, mc * 128:(mc + 1) * 128].rearrange("(kc kp) m -> kp kc m", kp=128)
        p.wait("gpsimd", "tensor")
        p.dma("dgps", wcol[:].rearrange("pp (kc m) -> pp kc m", m=128), src2)
        p.wait("tensor", "dgps")
        for tb in range(NT):
            pp = ps[tb % 2]
            sl = slice(tb * 512, (tb + 1) * 512)
            if mc * NT + tb >= 2:
                p.wait("tensor", "vector")
            for kc in range(NK):
                p.T("matmul", pp[:], wcol[:, kc * 128:(kc + 1) * 128],
                    fc[kc][:, sl], start=(kc == 0), stop=(kc == NK - 1))
            i = tb % 2
            p.wait("sync", "vector")
            p.dma("dsync", xfb[i][:], xf[mc * 128:(mc + 1) * 128, sl])
            p.dma("dsync", hb[i][:], hsv[mc * 128:(mc + 1) * 128, sl])
            p.wait("vector", "dsync")
            p.wait("vector", "tensor")
            p.V("tensor_tensor", out=ob[i][:], in0=pp[:], in1=xfb[i][:], op=ALU.add)
            p.V("tensor_tensor", out=ob[i][:], in0=ob[i][:], in1=hb[i][:], op=ALU.add)
            p.wait("sync", "vector")
            p.dma("dsync", xo[mc * 128:(mc + 1) * 128, sl], ob[i][:])
    p.wait("sync", "dsync")



def _mk(emit_fn):
    def b():
        p = Prog()
        emit_fn(p)
        return p.finish()
    return b


build_l0a_pre2 = _mk(emit_l0a_pre2)
build_l0a_scan = _mk(emit_l0a_scan)
build_l0b_attn = _mk(emit_l0b_attn)
build_m2_pre2 = _mk(emit_m2_pre2)
build_m2_scan = _mk(emit_m2_scan)
build_l1b_attn = _mk(emit_l1b_attn)


def build_gA():
    p = Prog()
    with p.stage():
        emit_w2prep(p)
    with p.stage():
        emit_l0a_pre1(p)
    return p.finish()


def build_gB():
    p = Prog()
    p.bind("xf1", p.idram("xf1_d", [D, TC]))
    with p.stage():
        emit_l0a_post(p)
    p._ext["xf"] = p._ext.pop("xf1")
    with p.stage():
        emit_l0ffn(p)
    p._ext["xf"] = p._ext["xo"]
    with p.stage():
        emit_l0b_qkv(p)
    return p.finish()


def build_gC():
    p = Prog()
    p.bind("xo", p.idram("xo_attn_d", [D, TC]))
    with p.stage():
        emit_l0b_post(p)
    p._ext["xf"] = p._ext.pop("xo")
    with p.stage():
        emit_l0ffn(p)
    p._ext["xf"] = p._ext["xo"]
    with p.stage():
        emit_m2_pre1(p)
    return p.finish()


def build_gD():
    p = Prog()
    p.bind("xo", p.idram("xo_m2_d", [D, TC]))
    with p.stage():
        emit_m2_post(p)
    p._ext["xf"] = p._ext.pop("xo")
    with p.stage():
        emit_l1ffn(p)
    p._ext["xf"] = p._ext["xo"]
    with p.stage():
        emit_l1b_pre(p)
    return p.finish()


def build_gE():
    p = Prog()
    p.bind("xo", p.idram("xo_l1b_d", [D, TC]))
    with p.stage():
        emit_l1b_post(p)
    p._ext["xf"] = p._ext.pop("xo")
    with p.stage():
        emit_l1ffn(p)
    return p.finish()



def _hadamard(n):
    H = np.array([[1.0]], np.float32)
    while H.shape[0] < n:
        H = np.block([[H, H], [H, -H]]).astype(np.float32)
    return H


def _causal_mask_l0():
    m = np.zeros((128, 4 * 512), np.float32)
    for diag in range(4):
        for kp in range(128):
            m[kp, diag * 512 + diag * 128 + kp:(diag + 1) * 512] = 1.0
    import ml_dtypes
    return np.ascontiguousarray(m.astype(ml_dtypes.bfloat16))


def _ltones_host():
    lt = np.zeros((128, 256), np.float32)
    for r in range(128):
        lt[r, r:128] = 1.0
    lt[:, 128:256] = 1.0
    return np.ascontiguousarray(lt)


def _cmask2_host():
    m = np.zeros((128, 512), np.float32)
    for s in range(128):
        m[s, s:256] = 1.0
        if 128 + s < 256:
            m[s, 256 + 128 + s:512] = 1.0
    import ml_dtypes
    return np.ascontiguousarray(m.astype(ml_dtypes.bfloat16))


def _nmask_host():
    m = np.zeros((128, 4 * 512), np.float32)
    for qm in range(4):
        for i in range(128):
            j0 = qm * 128 + i + 1
            if j0 < 512:
                m[i, qm * 512 + j0:(qm + 1) * 512] = -1e9
    return np.ascontiguousarray(m)


def _run(nc, maps):
    return run_bass_kernel_spmd(nc, maps, core_ids=list(range(NCORES)))


def _tok_maps(xf, extra):
    maps = []
    for c in range(NCORES):
        b, hh = c // 2, c % 2
        m = {"xf": np.ascontiguousarray(xf[b, :, hh * TC:(hh + 1) * TC])}
        m.update(extra)
        maps.append(m)
    return maps


def run_l0ffn_layer(xf, inp, consts):
    nc = build_l0ffn()
    wT = np.ascontiguousarray(inp["L0_ffn_weight"].T)
    biast = np.ascontiguousarray(inp["L0_ffn_bias"].reshape(16, 128).T)
    r = _run(nc, _tok_maps(xf, {"wT": wT, "biast": biast, "consts": consts}))
    xo = np.zeros_like(xf)
    _scatter_tok(xo, r.results, "xo")
    return xo


def run_l0attn_layer(xf, inp, consts):
    import ml_dtypes
    nc = build_l0b_qkv()
    wqT = np.ascontiguousarray(inp["L0_attn_b_wq"].T)
    wkT = np.ascontiguousarray(inp["L0_attn_b_wk"].T)
    wvT = np.ascontiguousarray(inp["L0_attn_b_wv"].T)
    r = _run(nc, _tok_maps(xf, {"wqT": wqT, "wkT": wkT, "wvT": wvT, "consts": consts}))
    qn = np.zeros((B, D, T), ml_dtypes.bfloat16)
    kn = np.zeros((B, D, T), ml_dtypes.bfloat16)
    vn = np.zeros((B, T, D), ml_dtypes.bfloat16)
    for c in range(NCORES):
        b, hh = c // 2, c % 2
        sl = slice(hh * TC, (hh + 1) * TC)
        qn[b, :, sl] = r.results[c]["qn"]
        kn[b, :, sl] = r.results[c]["kn"]
        vn[b, sl, :] = r.results[c]["vn"]

    nc = build_l0b_attn()
    cmask = _causal_mask_l0()
    tauv = np.ascontiguousarray(np.broadcast_to(
        np.asarray(inp["L0_attn_b_tau"], np.float32).reshape(1, 1), (128, 1)).copy())
    maps = []
    for c in range(NCORES):
        b, hg = c // 2, c % 2
        rows = slice(hg * 1024, (hg + 1) * 1024)
        maps.append({"qn": np.ascontiguousarray(qn[b, rows, :]),
                     "kn": np.ascontiguousarray(kn[b, rows, :]),
                     "vn": np.ascontiguousarray(vn[b, :, rows]),
                     "tauv": tauv, "cmask": cmask})
    r = _run(nc, maps)
    attno = np.zeros((B, D, T), np.float32)
    for c in range(NCORES):
        b, hg = c // 2, c % 2
        attno[b, hg * 1024:(hg + 1) * 1024, :] = r.results[c]["attno"]

    nc = build_l0b_post()
    nuv = np.ascontiguousarray(np.broadcast_to(
        np.asarray(inp["L0_attn_b_nu"], np.float32).reshape(1, 1), (128, 1)).copy())
    maps = []
    for c in range(NCORES):
        b, hh = c // 2, c % 2
        sl = slice(hh * TC, (hh + 1) * TC)
        maps.append({"att": np.ascontiguousarray(attno[b, :, sl]), "nuv": nuv,
                     "xf": np.ascontiguousarray(xf[b, :, sl])})
    r = _run(nc, maps)
    xo = np.zeros_like(xf)
    _scatter_tok(xo, r.results, "xo")
    return xo


def run_m2_layer(xf, inp, consts):
    import ml_dtypes
    nc = build_m2_pre1()
    wip = np.ascontiguousarray(inp["L1_attn_a_in_proj_w"])
    r = _run(nc, _tok_maps(xf, {"wip": wip, "consts": consts}))
    gate = np.zeros((B, M2_IN, T), np.float32)
    hsraw = np.zeros((B, M2_CONV, T), np.float32)
    dtr2 = np.zeros((B, 64, T), np.float32)
    _scatter_tok(gate, r.results, "gate")
    _scatter_tok(hsraw, r.results, "hsraw")
    _scatter_tok(dtr2, r.results, "dtr2")

    nc = build_m2_pre2()
    cw = inp["L1_attn_a_conv_w"]
    cb = inp["L1_attn_a_conv_b"]
    convw2 = np.ascontiguousarray(cw.reshape(34, 128, 4).transpose(1, 0, 2).reshape(128, 136))
    convb2 = np.ascontiguousarray(np.asarray(cb, np.float32).reshape(34, 128).T)
    dtbias = np.ascontiguousarray(np.asarray(inp["L1_attn_a_dt_bias"], np.float32).reshape(64, 1))
    hpad = np.zeros((B, M2_CONV, T + 3), np.float32)
    hpad[:, :, 3:] = hsraw
    maps = []
    for c in range(NCORES):
        b, hh = c // 2, c % 2
        t0 = hh * TC
        maps.append({"hsx": np.ascontiguousarray(hpad[b, :, t0:t0 + TC + 3]),
                     "convw2": convw2, "convb2": convb2,
                     "dtr2": np.ascontiguousarray(dtr2[b, :, t0:t0 + TC]),
                     "dtbias": dtbias, "consts": consts})
    r = _run(nc, maps)
    xs = np.zeros((B, M2_IN, T), ml_dtypes.bfloat16)
    bm = np.zeros((B, 128, T), ml_dtypes.bfloat16)
    cm = np.zeros((B, 128, T), ml_dtypes.bfloat16)
    dtv2 = np.zeros((B, 64, T), np.float32)
    for c in range(NCORES):
        b, hh = c // 2, c % 2
        sl = slice(hh * TC, (hh + 1) * TC)
        xs[b, :, sl] = r.results[c]["xs"]
        bm[b, :, sl] = r.results[c]["bm"]
        cm[b, :, sl] = r.results[c]["cm"]
        dtv2[b, :, sl] = r.results[c]["dtv2"]

    nc = build_m2_scan()
    alog = np.asarray(inp["L1_attn_a_A_log"], np.float32)
    id128f = np.eye(128, dtype=np.float32)
    id128b = np.ascontiguousarray(id128f.astype(ml_dtypes.bfloat16))
    maps = []
    for c in range(NCORES):
        b, hh = c // 2, c % 2
        rows = slice(hh * 2048, (hh + 1) * 2048)
        hrows = slice(hh * 32, (hh + 1) * 32)
        maps.append({"xs_h": np.ascontiguousarray(xs[b, rows, :]),
                     "bm": np.ascontiguousarray(bm[b]),
                     "cm": np.ascontiguousarray(cm[b]),
                     "dtv2h": np.ascontiguousarray(dtv2[b, hrows, :]),
                     "alog2": np.ascontiguousarray(alog[hrows].reshape(1, 32)),
                     "ltones": _ltones_host(), "id128b": id128b, "id128f": id128f,
                     "cmask256": _cmask2_host()})
    r = _run(nc, maps)
    y2 = np.zeros((B, M2_IN, T), np.float32)
    for c in range(NCORES):
        b, hh = c // 2, c % 2
        y2[b, hh * 2048:(hh + 1) * 2048, :] = r.results[c]["y2"]

    nc = build_m2_post()
    Dp = np.asarray(inp["L1_attn_a_D"], np.float32)
    drep = np.ascontiguousarray(np.repeat(Dp, 64).reshape(32, 128).T)
    nrmw = np.ascontiguousarray(np.asarray(inp["L1_attn_a_norm_w"], np.float32).reshape(32, 128).T)
    wop2 = np.ascontiguousarray(inp["L1_attn_a_out_proj_w"])
    maps = []
    for c in range(NCORES):
        b, hh = c // 2, c % 2
        sl = slice(hh * TC, (hh + 1) * TC)
        maps.append({"y2t": np.ascontiguousarray(y2[b, :, sl]),
                     "xst": np.ascontiguousarray(xs[b, :, sl]),
                     "gate": np.ascontiguousarray(gate[b, :, sl]),
                     "drep": drep, "nrmw": nrmw, "wop2": wop2,
                     "xf": np.ascontiguousarray(xf[b, :, sl]), "consts": consts})
    r = _run(nc, maps)
    xo = np.zeros_like(xf)
    _scatter_tok(xo, r.results, "xo")
    return xo


def run_l1ffn_layer(xf, inp, consts, w2p):
    import ml_dtypes
    nc = build_l1ffn()
    w1T = np.ascontiguousarray(inp["L1_ffn_W1"].T)
    w3T = np.ascontiguousarray(inp["L1_ffn_W3"].T)
    h128 = np.ascontiguousarray(_hadamard(128).astype(ml_dtypes.bfloat16))
    id128f = np.eye(128, dtype=np.float32)
    r = _run(nc, _tok_maps(xf, {"w1T": w1T, "w3T": w3T, "w2p": w2p,
                                "h128": h128, "id128f": id128f, "consts": consts}))
    xo = np.zeros_like(xf)
    _scatter_tok(xo, r.results, "xo")
    return xo


def run_l1attn_layer(xf, inp, consts):
    import ml_dtypes
    nc = build_l1b_pre()
    expT = np.ascontiguousarray(inp["L1_attn_b_expand"].T)
    gamt = np.ascontiguousarray(np.asarray(inp["L1_attn_b_ln_gamma"], np.float32).reshape(16, 128).T)
    bett = np.ascontiguousarray(np.asarray(inp["L1_attn_b_ln_beta"], np.float32).reshape(16, 128).T)
    r = _run(nc, _tok_maps(xf, {"expT": expT, "gamt": gamt, "bett": bett,
                                "consts": consts}))
    qg = np.zeros((B, 256, T), ml_dtypes.bfloat16)
    kg = np.zeros((B, 256, T), ml_dtypes.bfloat16)
    gl = np.zeros((B, D, T), ml_dtypes.bfloat16)
    vt = np.zeros((B, T, D), ml_dtypes.bfloat16)
    hsv = np.zeros((B, D, T), np.float32)
    for c in range(NCORES):
        b, hh = c // 2, c % 2
        sl = slice(hh * TC, (hh + 1) * TC)
        qg[b, :, sl] = r.results[c]["qg"]
        kg[b, :, sl] = r.results[c]["kg"]
        gl[b, :, sl] = r.results[c]["gl"]
        vt[b, sl, :] = r.results[c]["vt"]
        hsv[b, :, sl] = r.results[c]["hsv"]

    nc = build_l1b_attn()
    iotac = np.arange(128, dtype=np.float32).reshape(128, 1)
    nmask = _nmask_host()
    id128f = np.eye(128, dtype=np.float32)
    id128b = np.ascontiguousarray(id128f.astype(ml_dtypes.bfloat16))
    pmv = np.ascontiguousarray(np.asarray(inp["L1_attn_b_pos_mult"], np.float32).reshape(1, 1))
    maps = []
    for c in range(NCORES):
        b, vh = c // 2, c % 2
        maps.append({"qg": np.ascontiguousarray(qg[b]), "kg": np.ascontiguousarray(kg[b]),
                     "vth": np.ascontiguousarray(vt[b, :, vh * 1024:(vh + 1) * 1024]),
                     "pmv": pmv, "iotac": iotac, "nmask": nmask,
                     "id128b": id128b, "id128f": id128f, "consts": consts})
    r = _run(nc, maps)
    at1 = np.zeros((B, D, T), ml_dtypes.bfloat16)
    for c in range(NCORES):
        b, vh = c // 2, c % 2
        at1[b, vh * 1024:(vh + 1) * 1024, :] = r.results[c]["attn1"]

    nc = build_l1b_post()
    prjT = np.ascontiguousarray(inp["L1_attn_b_project"].T)
    maps = []
    for c in range(NCORES):
        b, hh = c // 2, c % 2
        sl = slice(hh * TC, (hh + 1) * TC)
        maps.append({"glt": np.ascontiguousarray(gl[b, :, sl]),
                     "at1": np.ascontiguousarray(at1[b, :, sl]),
                     "hsv": np.ascontiguousarray(hsv[b, :, sl]),
                     "xf": np.ascontiguousarray(xf[b, :, sl]), "prjT": prjT})
    r = _run(nc, maps)
    xo = np.zeros_like(xf)
    _scatter_tok(xo, r.results, "xo")
    return xo


def kernel(**inputs):
    import ml_dtypes
    consts = _consts_host()
    xf = np.ascontiguousarray(np.transpose(
        np.asarray(inputs["L0_attn_a_hidden_states"], np.float32), (0, 2, 1)))

    w2T = np.ascontiguousarray(inputs["L1_ffn_W2"].T)
    wipT = np.ascontiguousarray(inputs["L0_attn_a_in_proj_w"].T)
    maps = []
    for c in range(NCORES):
        b, hh = c // 2, c % 2
        maps.append({"w2T": np.ascontiguousarray(w2T[:, c * 256:(c + 1) * 256]),
                     "xin": np.ascontiguousarray(xf[b, :, hh * TC:(hh + 1) * TC]),
                     "wipT": wipT, "consts": consts})
    r = _run(build_gA(), maps)
    w2p = np.zeros((8192, D), ml_dtypes.bfloat16)
    for c in range(NCORES):
        w2p[:, c * 256:(c + 1) * 256] = r.results[c]["w2p"]
    w2p = np.ascontiguousarray(w2p)
    xz = np.zeros((B, 8192, T), np.float32)
    _scatter_tok(xz, r.results, "xz")

    convw = np.ascontiguousarray(
        inputs["L0_attn_a_conv_w"].reshape(32, 128, 4).transpose(1, 0, 2).reshape(128, 128))
    convb = np.ascontiguousarray(inputs["L0_attn_a_conv_b"].reshape(32, 128).T)
    xpT = np.ascontiguousarray(inputs["L0_attn_a_x_proj_w"].T)
    dtpT = np.ascontiguousarray(inputs["L0_attn_a_dt_proj_w"].T)
    dtpb = np.ascontiguousarray(inputs["L0_attn_a_dt_proj_b"].reshape(32, 128).T)
    xpad = np.zeros((B, DI, T + 3), np.float32)
    xpad[:, :, 3:] = xz[:, 0:DI, :]
    maps = []
    for c in range(NCORES):
        b, hh = c // 2, c % 2
        t0 = hh * TC
        maps.append({"xzx": np.ascontiguousarray(xpad[b, :, t0:t0 + TC + 3]),
                     "convw": convw, "convb": convb, "xpT": xpT,
                     "dtpT": dtpT, "dtpb": dtpb, "consts": consts})
    r = _run(build_l0a_pre2(), maps)
    xact = np.zeros((B, DI, T), np.float32)
    uu = np.zeros((B, DI, T), np.float32)
    bc = np.zeros((B, 32, T), np.float32)
    dtv = np.zeros((B, DI, T), np.float32)
    _scatter_tok(xact, r.results, "xact")
    _scatter_tok(uu, r.results, "uu")
    _scatter_tok(bc, r.results, "bc")
    _scatter_tok(dtv, r.results, "dtv")

    alog = np.asarray(inputs["L0_attn_a_A_log"], np.float32)
    sel16 = _sel16_host()
    maps = []
    for c in range(NCORES):
        b, dh = c // 2, c % 2
        rows = slice(dh * DH, (dh + 1) * DH)
        al = np.ascontiguousarray(alog[rows].reshape(256, 128).T)
        maps.append({"dtv_h": np.ascontiguousarray(dtv[b, rows]),
                     "uu_h": np.ascontiguousarray(uu[b, rows]),
                     "bc_all": np.ascontiguousarray(bc[b]),
                     "alog_l": al, "sel16": sel16})
    r = _run(build_l0a_scan(), maps)
    ysc = np.zeros((B, DI, T), np.float32)
    for c in range(NCORES):
        b, dh = c // 2, c % 2
        ysc[b, dh * DH:(dh + 1) * DH, :] = r.results[c]["ysc"]

    wopT = np.ascontiguousarray(inputs["L0_attn_a_out_proj_w"].T)
    dvec = np.ascontiguousarray(inputs["L0_attn_a_D"].reshape(32, 128).T)
    wT = np.ascontiguousarray(inputs["L0_ffn_weight"].T)
    biast = np.ascontiguousarray(inputs["L0_ffn_bias"].reshape(16, 128).T)
    wqT = np.ascontiguousarray(inputs["L0_attn_b_wq"].T)
    wkT = np.ascontiguousarray(inputs["L0_attn_b_wk"].T)
    wvT = np.ascontiguousarray(inputs["L0_attn_b_wv"].T)
    maps = []
    for c in range(NCORES):
        b, hh = c // 2, c % 2
        sl = slice(hh * TC, (hh + 1) * TC)
        maps.append({"ysc_t": np.ascontiguousarray(ysc[b, :, sl]),
                     "xact_t": np.ascontiguousarray(xact[b, :, sl]),
                     "zz": np.ascontiguousarray(xz[b, DI:2 * DI, sl]),
                     "dvec": dvec, "wopT": wopT,
                     "xf": np.ascontiguousarray(xf[b, :, sl]),
                     "wT": wT, "biast": biast, "consts": consts,
                     "wqT": wqT, "wkT": wkT, "wvT": wvT})
    r = _run(build_gB(), maps)
    xffn = np.zeros_like(xf)
    _scatter_tok(xffn, r.results, "xo")
    qn = np.zeros((B, D, T), ml_dtypes.bfloat16)
    kn = np.zeros((B, D, T), ml_dtypes.bfloat16)
    vn = np.zeros((B, T, D), ml_dtypes.bfloat16)
    for c in range(NCORES):
        b, hh = c // 2, c % 2
        sl = slice(hh * TC, (hh + 1) * TC)
        qn[b, :, sl] = r.results[c]["qn"]
        kn[b, :, sl] = r.results[c]["kn"]
        vn[b, sl, :] = r.results[c]["vn"]

    cmask = _causal_mask_l0()
    tauv = np.ascontiguousarray(np.broadcast_to(
        np.asarray(inputs["L0_attn_b_tau"], np.float32).reshape(1, 1), (128, 1)).copy())
    maps = []
    for c in range(NCORES):
        b, hg = c // 2, c % 2
        rows = slice(hg * 1024, (hg + 1) * 1024)
        maps.append({"qn": np.ascontiguousarray(qn[b, rows, :]),
                     "kn": np.ascontiguousarray(kn[b, rows, :]),
                     "vn": np.ascontiguousarray(vn[b, :, rows]),
                     "tauv": tauv, "cmask": cmask})
    r = _run(build_l0b_attn(), maps)
    attno = np.zeros((B, D, T), np.float32)
    for c in range(NCORES):
        b, hg = c // 2, c % 2
        attno[b, hg * 1024:(hg + 1) * 1024, :] = r.results[c]["attno"]

    nuv = np.ascontiguousarray(np.broadcast_to(
        np.asarray(inputs["L0_attn_b_nu"], np.float32).reshape(1, 1), (128, 1)).copy())
    wip = np.ascontiguousarray(inputs["L1_attn_a_in_proj_w"])
    maps = []
    for c in range(NCORES):
        b, hh = c // 2, c % 2
        sl = slice(hh * TC, (hh + 1) * TC)
        maps.append({"att": np.ascontiguousarray(attno[b, :, sl]), "nuv": nuv,
                     "xf": np.ascontiguousarray(xffn[b, :, sl]),
                     "wT": wT, "biast": biast, "consts": consts, "wip": wip})
    r = _run(build_gC(), maps)
    xm2 = np.zeros_like(xf)
    _scatter_tok(xm2, r.results, "xo")
    gate = np.zeros((B, M2_IN, T), np.float32)
    hsraw = np.zeros((B, M2_CONV, T), np.float32)
    dtr2 = np.zeros((B, 64, T), np.float32)
    _scatter_tok(gate, r.results, "gate")
    _scatter_tok(hsraw, r.results, "hsraw")
    _scatter_tok(dtr2, r.results, "dtr2")

    cw = inputs["L1_attn_a_conv_w"]
    cb = inputs["L1_attn_a_conv_b"]
    convw2 = np.ascontiguousarray(cw.reshape(34, 128, 4).transpose(1, 0, 2).reshape(128, 136))
    convb2 = np.ascontiguousarray(np.asarray(cb, np.float32).reshape(34, 128).T)
    dtbias = np.ascontiguousarray(np.asarray(inputs["L1_attn_a_dt_bias"], np.float32).reshape(64, 1))
    hpad = np.zeros((B, M2_CONV, T + 3), np.float32)
    hpad[:, :, 3:] = hsraw
    maps = []
    for c in range(NCORES):
        b, hh = c // 2, c % 2
        t0 = hh * TC
        maps.append({"hsx": np.ascontiguousarray(hpad[b, :, t0:t0 + TC + 3]),
                     "convw2": convw2, "convb2": convb2,
                     "dtr2": np.ascontiguousarray(dtr2[b, :, t0:t0 + TC]),
                     "dtbias": dtbias, "consts": consts})
    r = _run(build_m2_pre2(), maps)
    xs = np.zeros((B, M2_IN, T), ml_dtypes.bfloat16)
    bm = np.zeros((B, 128, T), ml_dtypes.bfloat16)
    cm = np.zeros((B, 128, T), ml_dtypes.bfloat16)
    dtv2 = np.zeros((B, 64, T), np.float32)
    for c in range(NCORES):
        b, hh = c // 2, c % 2
        sl = slice(hh * TC, (hh + 1) * TC)
        xs[b, :, sl] = r.results[c]["xs"]
        bm[b, :, sl] = r.results[c]["bm"]
        cm[b, :, sl] = r.results[c]["cm"]
        dtv2[b, :, sl] = r.results[c]["dtv2"]

    alog = np.asarray(inputs["L1_attn_a_A_log"], np.float32)
    id128f = np.eye(128, dtype=np.float32)
    id128b = np.ascontiguousarray(id128f.astype(ml_dtypes.bfloat16))
    maps = []
    for c in range(NCORES):
        b, hh = c // 2, c % 2
        rows = slice(hh * 2048, (hh + 1) * 2048)
        hrows = slice(hh * 32, (hh + 1) * 32)
        maps.append({"xs_h": np.ascontiguousarray(xs[b, rows, :]),
                     "bm": np.ascontiguousarray(bm[b]),
                     "cm": np.ascontiguousarray(cm[b]),
                     "dtv2h": np.ascontiguousarray(dtv2[b, hrows, :]),
                     "alog2": np.ascontiguousarray(alog[hrows].reshape(1, 32)),
                     "ltones": _ltones_host(), "id128b": id128b, "id128f": id128f,
                     "cmask256": _cmask2_host()})
    r = _run(build_m2_scan(), maps)
    y2 = np.zeros((B, M2_IN, T), np.float32)
    for c in range(NCORES):
        b, hh = c // 2, c % 2
        y2[b, hh * 2048:(hh + 1) * 2048, :] = r.results[c]["y2"]

    Dp = np.asarray(inputs["L1_attn_a_D"], np.float32)
    drep = np.ascontiguousarray(np.repeat(Dp, 64).reshape(32, 128).T)
    nrmw = np.ascontiguousarray(np.asarray(inputs["L1_attn_a_norm_w"], np.float32).reshape(32, 128).T)
    wop2 = np.ascontiguousarray(inputs["L1_attn_a_out_proj_w"])
    w1T = np.ascontiguousarray(inputs["L1_ffn_W1"].T)
    w3T = np.ascontiguousarray(inputs["L1_ffn_W3"].T)
    h128 = np.ascontiguousarray(_hadamard(128).astype(ml_dtypes.bfloat16))
    expT = np.ascontiguousarray(inputs["L1_attn_b_expand"].T)
    gamt = np.ascontiguousarray(np.asarray(inputs["L1_attn_b_ln_gamma"], np.float32).reshape(16, 128).T)
    bett = np.ascontiguousarray(np.asarray(inputs["L1_attn_b_ln_beta"], np.float32).reshape(16, 128).T)
    maps = []
    for c in range(NCORES):
        b, hh = c // 2, c % 2
        sl = slice(hh * TC, (hh + 1) * TC)
        maps.append({"y2t": np.ascontiguousarray(y2[b, :, sl]),
                     "xst": np.ascontiguousarray(xs[b, :, sl]),
                     "gate": np.ascontiguousarray(gate[b, :, sl]),
                     "drep": drep, "nrmw": nrmw, "wop2": wop2,
                     "xf": np.ascontiguousarray(xm2[b, :, sl]), "consts": consts,
                     "w1T": w1T, "w3T": w3T, "w2p": w2p, "h128": h128,
                     "id128f": id128f,
                     "expT": expT, "gamt": gamt, "bett": bett})
    r = _run(build_gD(), maps)
    xffn2 = np.zeros_like(xf)
    _scatter_tok(xffn2, r.results, "xo")
    qg = np.zeros((B, 256, T), ml_dtypes.bfloat16)
    kg = np.zeros((B, 256, T), ml_dtypes.bfloat16)
    gl = np.zeros((B, D, T), ml_dtypes.bfloat16)
    vt = np.zeros((B, T, D), ml_dtypes.bfloat16)
    hsv = np.zeros((B, D, T), np.float32)
    for c in range(NCORES):
        b, hh = c // 2, c % 2
        sl = slice(hh * TC, (hh + 1) * TC)
        qg[b, :, sl] = r.results[c]["qg"]
        kg[b, :, sl] = r.results[c]["kg"]
        gl[b, :, sl] = r.results[c]["gl"]
        vt[b, sl, :] = r.results[c]["vt"]
        hsv[b, :, sl] = r.results[c]["hsv"]

    iotac = np.arange(128, dtype=np.float32).reshape(128, 1)
    nmask = _nmask_host()
    pmv = np.ascontiguousarray(np.asarray(inputs["L1_attn_b_pos_mult"], np.float32).reshape(1, 1))
    maps = []
    for c in range(NCORES):
        b, vh = c // 2, c % 2
        maps.append({"qg": np.ascontiguousarray(qg[b]), "kg": np.ascontiguousarray(kg[b]),
                     "vth": np.ascontiguousarray(vt[b, :, vh * 1024:(vh + 1) * 1024]),
                     "pmv": pmv, "iotac": iotac, "nmask": nmask,
                     "id128b": id128b, "id128f": id128f, "consts": consts})
    r = _run(build_l1b_attn(), maps)
    at1 = np.zeros((B, D, T), ml_dtypes.bfloat16)
    for c in range(NCORES):
        b, vh = c // 2, c % 2
        at1[b, vh * 1024:(vh + 1) * 1024, :] = r.results[c]["attn1"]

    prjT = np.ascontiguousarray(inputs["L1_attn_b_project"].T)
    maps = []
    for c in range(NCORES):
        b, hh = c // 2, c % 2
        sl = slice(hh * TC, (hh + 1) * TC)
        maps.append({"glt": np.ascontiguousarray(gl[b, :, sl]),
                     "at1": np.ascontiguousarray(at1[b, :, sl]),
                     "hsv": np.ascontiguousarray(hsv[b, :, sl]),
                     "xf": np.ascontiguousarray(xffn2[b, :, sl]), "prjT": prjT,
                     "w1T": w1T, "w3T": w3T, "w2p": w2p, "h128": h128,
                     "id128f": id128f, "consts": consts})
    r = _run(build_gE(), maps)
    xout = np.zeros_like(xf)
    _scatter_tok(xout, r.results, "xo")
    out = np.ascontiguousarray(np.transpose(xout, (0, 2, 1)))
    return out
